# version 36
# baseline (speedup 1.0000x reference)
import os
import numpy as np
import ml_dtypes
from contextlib import ExitStack

import concourse.bass as bass
import concourse.mybir as mybir
from concourse.bass_utils import run_bass_kernel_spmd

F32 = mybir.dt.float32
BF16 = mybir.dt.bfloat16
AF = mybir.ActivationFunctionType
ALU = mybir.AluOpType
AX = mybir.AxisListType

NCORE = 8
T = 2048
D = 1024
DK = 128
DFF = 4096
NSEQ = 2
NTOK = NSEQ * T
NCD = D // 128
NCF = DFF // 128
NT1 = T // 128
R1 = 512
NR1 = T // R1
R3 = 256
NR3 = T // R3
QSCALE = float(DK) ** -0.5
BS = 12.0
LNEPS = 1e-5


class Ctr:

    def __init__(self, h):
        self.h = h
        self.n = 0

    def inc(self, ins, k=1):
        ins.then_inc(self.h, k)
        self.n += k
        return self.n


def build_nc():
    nc = bass.Bass("TRN2", target_bir_lowering=False)

    x_in = nc.dram_tensor("x", [NTOK, D], F32, kind="ExternalInput").ap()
    w1 = nc.dram_tensor("w1", [D, DFF], F32, kind="ExternalInput").ap()
    w2 = nc.dram_tensor("w2", [DFF, D], F32, kind="ExternalInput").ap()
    wq = nc.dram_tensor("wq", [D, DK], F32, kind="ExternalInput").ap()
    wk = nc.dram_tensor("wk", [D, DK], F32, kind="ExternalInput").ap()
    wv = nc.dram_tensor("wv", [D, DK], F32, kind="ExternalInput").ap()
    wo = nc.dram_tensor("wo", [DK, D], F32, kind="ExternalInput").ap()
    b1t = nc.dram_tensor("b1t", [128, NCF], F32, kind="ExternalInput").ap()
    tht = nc.dram_tensor("tht", [128, NCF], F32, kind="ExternalInput").ap()
    wrt = nc.dram_tensor("wrt", [128, NCF], F32, kind="ExternalInput").ap()
    wit = nc.dram_tensor("wit", [128, NCF], F32, kind="ExternalInput").ap()
    b2rep = nc.dram_tensor("b2rep", [128, D], F32, kind="ExternalInput").ap()
    maskd = nc.dram_tensor("maskd", [128, 128], BF16, kind="ExternalInput").ap()
    idn = nc.dram_tensor("idn", [128, 128], BF16, kind="ExternalInput").ap()
    ones1 = nc.dram_tensor("ones1", [1, 128], BF16, kind="ExternalInput").ap()
    fcst = nc.dram_tensor("fcst", [128, 8], F32, kind="ExternalInput").ap()
    out = nc.dram_tensor("out", [NTOK, D], F32, kind="ExternalOutput").ap()
    _xnkind = "ExternalOutput" if os.environ.get("KDBG_XN") else "Internal"
    xn_scr = [nc.dram_tensor(f"xn_scr{i}", [128, NCD, T], BF16, kind=_xnkind).ap()
              for i in range(NSEQ)]

    with ExitStack() as ctx:
        sbuf = lambda name, shape, dt: ctx.enter_context(nc.sbuf_tensor(name, shape, dt))
        sem = lambda name: Ctr(ctx.enter_context(nc.semaphore(name)))

        W1_sb = sbuf("W1_sb", [128, NCD * DFF], BF16)
        W2_sb = sbuf("W2_sb", [128, NCF * D], BF16)
        Wq_sb = sbuf("Wq_sb", [128, NCD * DK], BF16)
        Wk_sb = sbuf("Wk_sb", [128, NCD * DK], BF16)
        Wv_sb = sbuf("Wv_sb", [128, NCD * DK], BF16)
        Wo_sb = sbuf("Wo_sb", [128, D], BF16)
        b1_sb = sbuf("b1_sb", [128, NCF], F32)
        th_sb = sbuf("th_sb", [128, NCF], F32)
        wr_sb = sbuf("wr_sb", [128, NCF], F32)
        wi_sb = sbuf("wi_sb", [128, NCF], F32)
        Rp_sb = sbuf("Rp_sb", [128, NCF], F32)
        phb_sb = sbuf("phb_sb", [128, NCF], F32)
        prep = sbuf("prep", [128, 8 * NCF], F32)
        mask_sb = sbuf("mask_sb", [128, 128], BF16)
        idn_sb = sbuf("idn_sb", [128, 128], BF16)
        ones_sb = sbuf("ones_sb", [1, 128], BF16)
        diag_li = [sbuf(f"diag_li{i}", [128, 128], BF16) for i in range(2)]
        fc_sb = sbuf("fc_sb", [128, 8], F32)
        stats = sbuf("stats", [128, 64], F32)

        x_sb = [sbuf(f"x_sb{i}", [128, D], F32) for i in range(3)]

        xn_rg = sbuf("xn_rg", [128, NCD * R1], BF16)
        Q_fm = sbuf("Q_fm", [128, T], BF16)
        b2q = Q_fm[:].bitcast(F32)
        K_fm = sbuf("K_fm", [128, T], BF16)
        V_tm = sbuf("V_tm", [128, NT1 * DK], BF16)
        o_fm2 = [sbuf(f"o_fm{i}", [128, T], BF16) for i in range(2)]
        probs = sbuf("probs", [128, T], BF16)
        xn_tm = [probs[:, 0:D], probs[:, D:2 * D], None]
        pT_big = [sbuf(f"pT_big{i}", [128, 1024], BF16) for i in range(2)]
        xn_tm[2] = pT_big[1][:]
        xn_r3 = [sbuf(f"xn_r3{i}", [128, NCD * R3], BF16) for i in range(2)]
        x_sb.append(xn_r3[0][:].bitcast(F32))
        x_sb.append(xn_r3[1][:].bitcast(F32))
        rb_sb = [sbuf(f"rb_sb{i}", [128, R3], BF16) for i in range(3)]
        ss_sb = [sbuf(f"ss_sb{i}", [128, R3], BF16) for i in range(2)]
        u2_sb = [sbuf(f"u2_sb{i}", [128, R3], F32) for i in range(2)]
        out_sb = sbuf("out_sb", [128, D], F32)
        out_sb2 = [out_sb, probs[:].bitcast(F32)]

        sc0 = sem("sc0")
        swq = sem("swq")
        swr = sem("swr")
        sbar = sem("sbar")
        sxp = [sem(f"sxp{i}") for i in range(5)]
        ssq = sem("ssq")
        svar = sem("svar")
        srst = sem("srst")
        sln = sem("sln")
        sdg3 = sem("sdg3")
        stx = sem("stx")
        stxe = sem("stxe")
        sxnst = [sem(f"sxnst{i}") for i in range(4)]
        sqk = sem("sqk")
        sqke = sem("sqke")
        slg = sem("slg")
        smd = sem("smd")
        se1 = sem("se1")
        se1B = sem("se1B")
        se2 = sem("se2")
        stx2 = sem("stx2")
        slgA = sem("slgA")
        st2e = sem("st2e")
        st2eV = sem("st2eV")
        spv = sem("spv")
        soe = sem("soe")
        sxn3p = [sem("sxn3p0"), sem("sxn3p1")]
        sh = sem("sh")
        ssc3 = sem("ssc3")
        su2 = sem("su2")
        srb = sem("srb")
        sw2 = sem("sw2")
        soadd = sem("soadd")
        sost = [[sem(f"sost{h}{p}") for p in range(2)] for h in range(2)]
        sp1 = sem("sp1")
        sp2 = sem("sp2")
        sp3 = sem("sp3")
        sb2 = sem("sb2")
        sprep = sem("sprep")

        BAR = [0]
        _plim = int(os.environ.get("KDBG_PHASES", "99"))

        class _Done(Exception):
            pass

        with nc.Block(no_gpsimd_drain=True) as block:
            @block.sync
            def _(sync):
                for dst, src in [(fc_sb, fcst), (idn_sb, idn), (mask_sb, maskd),
                                 (ones_sb, ones1), (b1_sb, b1t), (th_sb, tht),
                                 (wr_sb, wrt), (wi_sb, wit)]:
                    sc0.inc(sync.dma_start(dst[:], src), 16)

            @block.gpsimd
            def _(g):
                for w_hbm, w_sb in [(wq, Wq_sb), (wk, Wk_sb), (wv, Wv_sb)]:
                    src = w_hbm.rearrange("(c p) k -> p c k", c=NCD, p=128)
                    dst = w_sb[:].rearrange("p (c k) -> p c k", c=NCD)
                    swq.inc(g.dma_start(dst, src), 16)
                swr.inc(g.dma_start(Wo_sb[:], wo), 16)


        try:
          for s in range(NSEQ):
            xb = s * T
            if 3 * s + 1 > _plim:
                raise _Done()
            with ExitStack() as pctx:
                psa = lambda name, shape, dt: pctx.enter_context(
                    nc.psum_tensor(name, shape, dt))
                psTxB = [psa(f"psTx{s}_{i}", [128, 1024], F32) for i in range(2)]
                psQ = psa(f"psQ{s}", [128, R1], F32)
                psK = psa(f"psK{s}", [128, R1], F32)
                psVB = psa(f"psVB{s}", [128, R1], F32)
                blk = pctx.enter_context(nc.Block(no_gpsimd_drain=True))

                sxp_b = [sxp[i].n for i in range(5)]
                ssq_b = ssq.n
                svar_b = svar.n
                srst_b = srst.n
                sln_b = sln.n
                sdg3_b = sdg3.n
                stx_b = stx.n
                stxe_b = stxe.n
                sxnst_b = [sxnst[i].n for i in range(4)]
                sqk_b = sqk.n
                sqke_b = sqke.n
                if s == 0:
                    sxn3p_pre_seq = {}
                bar_in = BAR[0]

                @blk.sync
                def _(sync):
                    if bar_in:
                        sync.wait_ge(sbar.h, bar_in)
                    for j in range(NT1):
                        if j >= 5:
                            sync.wait_ge(sln.h, sln_b + j - 4)
                            sync.wait_ge(svar.h, svar_b + j - 4)
                        sxp[j % 5].inc(sync.dma_start(
                            x_sb[j % 5][:], x_in[xb + j * 128: xb + (j + 1) * 128, :]), 16)
                    for pp in range(5):
                        sync.wait_ge(sxp[pp].h, sxp_b[pp] + 16 * len(
                            [j for j in range(NT1) if j % 5 == pp]))
                    sync.sem_inc(sbar.h, 1)

                @blk.gpsimd
                def _(g):
                    if bar_in:
                        g.wait_ge(sbar.h, bar_in)
                    for j in range(NT1):
                        g.wait_ge(stxe.h, stxe_b + j + 1)
                        src = xn_rg[:, (j % 4) * 1024:(j % 4 + 1) * 1024].rearrange(
                            "p (c n) -> p c n", c=NCD)
                        dst = xn_scr[s][:, :, j * 128:(j + 1) * 128]
                        sxnst[j % 4].inc(g.dma_start(dst, src), 16)
                    if s == 0:
                        for jw in range(4):
                            srcw = w1.rearrange("(a c p) f -> a p c f",
                                                a=4, c=2, p=128)[jw]
                            dstw = W1_sb[:, jw * 2 * DFF:(jw + 1) * 2 * DFF].rearrange(
                                "p (c f) -> p c f", c=2)
                            swr.inc(g.dma_start(dstw, srcw), 16)
                    else:
                        for jw in range(4):
                            srcw = w2.rearrange("(a c p) d -> a p c d",
                                                a=4, c=8, p=128)[jw]
                            dstw = W2_sb[:, jw * 8 * D:(jw + 1) * 8 * D].rearrange(
                                "p (c d) -> p c d", c=8)
                            swr.inc(g.dma_start(dstw, srcw), 16)
                    for i in range(4):
                        g.wait_ge(sxnst[i].h, sxnst_b[i] + 16 * NR1)
                    g.sem_inc(sbar.h, 1)

                @blk.vector
                def _(v):
                    if bar_in:
                        v.wait_ge(sbar.h, bar_in)
                    for j in range(NT1 + 2):
                        if j < NT1:
                            p2 = 16 * (j % 4)
                            v.wait_ge(sxp[j % 5].h, sxp_b[j % 5] + 16 * (j // 5 + 1))
                            v.bn_stats(stats[:, p2 + 0:p2 + 6], x_sb[j % 5][:, 0:512])
                            v.bn_stats(stats[:, p2 + 6:p2 + 12], x_sb[j % 5][:, 512:1024])
                            v.drain()
                            ins = v.bn_aggr(stats[:, p2 + 12:p2 + 14],
                                            stats[:, p2 + 0:p2 + 12])
                            svar.inc(ins)
                        jj = j - 2
                        if jj >= 0:
                            p2 = 16 * (jj % 4)
                            v.wait_ge(srst.h, srst_b + jj + 1)
                            v.reciprocal(stats[:, p2 + 14:p2 + 15],
                                         stats[:, p2 + 15:p2 + 16])
                            if jj >= 2:
                                v.wait_ge(stx.h, stx_b + jj - 1)
                            v.drain()
                            ins = v.tensor_scalar(
                                diag_li[jj % 2][:], idn_sb[:],
                                stats[:, p2 + 14:p2 + 15], -1.0,
                                ALU.mult, ALU.mult)
                            sdg3.inc(ins)
                    v.drain()
                    v.sem_inc(sbar.h, 1)

                @blk.scalar
                def _(sc):
                    if bar_in:
                        sc.wait_ge(sbar.h, bar_in)
                    sc.wait_ge(sc0.h, 128)
                    for j in range(NT1 + 2):
                        jj = j - 1
                        if 0 <= jj < NT1:
                            p2 = 16 * (jj % 4)
                            sc.wait_ge(svar.h, svar_b + jj + 1)
                            srst.inc(sc.activation(stats[:, p2 + 15:p2 + 16],
                                                   stats[:, p2 + 13:p2 + 14],
                                                   AF.Sqrt, bias=fc_sb[:, 0:1]))
                            if jj >= 3:
                                sc.wait_ge(stx.h, stx_b + jj - 2)
                            ins = sc.activation(xn_tm[jj % 3], x_sb[jj % 5][:],
                                                AF.Identity, scale=-1.0,
                                                bias=stats[:, p2 + 12:p2 + 13])
                            sln.inc(ins)
                        je = j - 2
                        if 0 <= je < NT1:
                            r, tt = je // 4, je % 4
                            if tt == 0 and r > 0:
                                sc.wait_ge(sqk.h, sqk_b + 4 * r)
                            if je >= 4:
                                sc.wait_ge(sxnst[je % 4].h,
                                           sxnst_b[je % 4] + 16 * (je // 4))
                            sc.wait_ge(stx.h, stx_b + je + 1)
                            stxe.inc(sc.copy(xn_rg[:, tt * 1024:(tt + 1) * 1024],
                                             psTxB[je % 2][:]))
                            if tt == 0 and r >= 1:
                                rp = r - 1
                                sc.wait_ge(sqk.h, sqk_b + 4 * (rp + 1))
                                sqke.inc(sc.activation(Q_fm[:, rp * R1:(rp + 1) * R1],
                                                       psQ[:], AF.Copy, scale=QSCALE))
                                sqke.inc(sc.copy(K_fm[:, rp * R1:(rp + 1) * R1], psK[:]))
                                sqke.inc(sc.copy(V_tm[:, rp * R1:(rp + 1) * R1], psVB[:]))

                    rp = NR1 - 1
                    sc.wait_ge(sqk.h, sqk_b + 4 * (rp + 1))
                    sqke.inc(sc.activation(Q_fm[:, rp * R1:(rp + 1) * R1],
                                           psQ[:], AF.Copy, scale=QSCALE))
                    sqke.inc(sc.copy(K_fm[:, rp * R1:(rp + 1) * R1], psK[:]))
                    sqke.inc(sc.copy(V_tm[:, rp * R1:(rp + 1) * R1], psVB[:]))
                    sc.drain()
                    sc.sem_inc(sbar.h, 1)

                @blk.tensor
                def _(t):
                    if bar_in:
                        t.wait_ge(sbar.h, bar_in)
                    t.wait_ge(sc0.h, 128)
                    if s == 0:
                        t.wait_ge(swq.h, 48)
                    def qkv_slice(jq):
                        tt, r = jq % 4, jq // 4
                        if tt == 0 and r > 0:
                            t.wait_ge(sqke.h, sqke_b + 3 * r)
                        t.wait_ge(stxe.h, stxe_b + jq + 1)
                        for ci in range(NCD):
                            t.matmul(psQ[:, tt * 128:(tt + 1) * 128],
                                     Wq_sb[:, ci * DK:(ci + 1) * DK],
                                     xn_rg[:, tt * 1024 + ci * 128:
                                            tt * 1024 + (ci + 1) * 128],
                                     start=(ci == 0), stop=(ci == NCD - 1),
                                     skip_group_check=True)
                        for ci in range(NCD):
                            t.matmul(psK[:, tt * 128:(tt + 1) * 128],
                                     Wk_sb[:, ci * DK:(ci + 1) * DK],
                                     xn_rg[:, tt * 1024 + ci * 128:
                                            tt * 1024 + (ci + 1) * 128],
                                     start=(ci == 0), stop=(ci == NCD - 1),
                                     skip_group_check=True)
                        for ci in range(NCD):
                            ins = t.matmul(psVB[:, tt * 128:(tt + 1) * 128],
                                           xn_rg[:, tt * 1024 + ci * 128:
                                                  tt * 1024 + (ci + 1) * 128],
                                           Wv_sb[:, ci * DK:(ci + 1) * DK],
                                           start=(ci == 0), stop=(ci == NCD - 1),
                                           skip_group_check=True)
                        sqk.inc(ins)

                    for j in range(NT1):
                        t.wait_ge(sln.h, sln_b + j + 1)
                        t.wait_ge(sdg3.h, sdg3_b + j + 1)
                        if j >= 2:
                            t.wait_ge(stxe.h, stxe_b + j - 1)
                        for c in range(NCD):
                            ins = t.matmul(psTxB[j % 2][:, c * 128:(c + 1) * 128],
                                           xn_tm[j % 3][:, c * 128:(c + 1) * 128],
                                           diag_li[j % 2][:],
                                           start=True, stop=True,
                                           skip_group_check=True)
                        stx.inc(ins)
                        if j >= 1:
                            qkv_slice(j - 1)
                    qkv_slice(NT1 - 1)
                    t.drain()
                    t.sem_inc(sbar.h, 1)

            BAR[0] += 5
            if 3 * s + 2 > _plim:
                raise _Done()

            with ExitStack() as pctx:
                psa = lambda name, shape, dt: pctx.enter_context(
                    nc.psum_tensor(name, shape, dt))
                psLogA = psa(f"psLogA{s}", [128, 1024], F32)
                psLogB = psa(f"psLogB{s}", [128, 1024], F32)
                psT2 = [psa(f"psT2{s}_{i}", [128, 512], F32) for i in range(2)]
                psO = [psa(f"psO{s}_{i}", [128, 128], F32) for i in range(2)]
                blk = pctx.enter_context(nc.Block())

                slg_b = slg.n
                se1_b = se1.n
                se1B_b = se1B.n
                se2_b = se2.n
                stx2_b = stx2.n
                st2e_b = st2e.n
                st2eV_b = st2eV.n
                spv_b = spv.n
                soe_b = soe.n
                bar_in = BAR[0]
                cum = [0]
                for qi in range(NT1):
                    cum.append(cum[-1] + qi + 1)

                slgA_b = slgA.n
                p2buf = [xn_rg[:, 0:T], xn_rg[:, T:2 * T]]
                nsg = [j // 4 + 1 for j in range(NT1)]
                cum_sg = [0]
                for j in range(NT1):
                    cum_sg.append(cum_sg[-1] + nsg[j])
                segs = [(j, si) for j in range(NT1) for si in range(nsg[j])]
                bank_of, ord_in_bank = {}, {}
                _cnt = [0, 0]
                for g, key in enumerate(segs):
                    b = g % 2
                    bank_of[key] = b
                    _cnt[b] += 1
                    ord_in_bank[key] = _cnt[b]
                segw2 = lambda j, si: (min(j + 1 - si * 4, 4)) * 128
                ev_sem = lambda b: (st2e, st2e_b) if b == 0 else (st2eV, st2eV_b)

                NF = NCF
                col = lambda i: prep[:, i * NF:(i + 1) * NF]
                pv = {}
                if s == 1:
                    vops = [lambda v: v.wait_ge(sp1.h, 2)]

                    def _o(fn):
                        vops.append(lambda v, fn=fn: (v.drain(), fn(v)))
                    _o(lambda v: v.tensor_scalar(col(1), col(6), -2.0, 1.0,
                                                 ALU.mult, ALU.add))
                    _o(lambda v: v.tensor_copy(col(0), col(5)))
                    for _ in range(4):
                        _o(lambda v: v.scalar_tensor_tensor(col(2), col(0), 2.0,
                                                            col(1), ALU.mult, ALU.mult))
                        _o(lambda v: v.tensor_tensor(col(3), col(0), col(0), ALU.mult))
                        _o(lambda v: v.tensor_scalar(col(1), col(3), -2.0, 1.0,
                                                     ALU.mult, ALU.add))
                        _o(lambda v: v.tensor_copy(col(0), col(2)))
                    _o(lambda v: v.tensor_tensor(col(2), col(1), wr_sb[:], ALU.mult))
                    _o(lambda v: v.tensor_tensor(col(3), col(0), wi_sb[:], ALU.mult))
                    _o(lambda v: v.tensor_tensor(col(2), col(2), col(3), ALU.add))
                    _o(lambda v: v.tensor_tensor(col(3), col(2), wi_sb[:], ALU.mult))
                    _o(lambda v: v.tensor_scalar(col(4), col(1), -1.0, None, ALU.mult))
                    _o(lambda v: v.tensor_tensor(col(4), col(1), col(4), ALU.max))
                    _o(lambda v: v.tensor_scalar(col(4), col(4), 1.0, None, ALU.add))
                    _o(lambda v: v.reciprocal(col(5), col(4)))
                    _o(lambda v: sp2.inc(v.tensor_tensor(col(4), col(0), col(5),
                                                         ALU.mult)))
                    for i, fn in enumerate(vops):
                        pv.setdefault(min(i // 3, 9), []).append(fn)

                    def _mid(v):
                        v.wait_ge(sp3.h, 1)
                        v.drain()
                        sp2.inc(v.tensor_tensor(col(5), col(4), col(7), ALU.mult))
                    pv.setdefault(13, []).append(_mid)

                    def _tail(v):
                        v.wait_ge(sp3.h, 2)
                        v.drain()
                        v.scalar_tensor_tensor(Rp_sb[:], col(3), 0.1, col(7),
                                               ALU.mult, ALU.mult)
                        v.drain()
                        v.tensor_scalar(col(5), col(6), 2.0, None, ALU.mult)
                        v.drain()
                        ins = v.scalar_tensor_tensor(phb_sb[:], b1_sb[:], 0.1,
                                                     col(5), ALU.mult, ALU.add)
                        sprep.inc(ins)
                    pv.setdefault(15, []).append(_tail)

                @blk.tensor
                def _(t):
                    t.wait_ge(sbar.h, bar_in)
                    for qq in range(NT1 + 1):
                        j = qq - 1
                        if qq < NT1:
                            qi = qq
                            if qi >= 1:
                                t.wait_ge(se1.h, se1_b + qi)
                            qsl = Q_fm[:, qi * 128:(qi + 1) * 128]
                            if qi >= 8:
                                for kb in range(2):
                                    ins = t.matmul(psLogA[:, kb * R1:(kb + 1) * R1],
                                                   qsl, K_fm[:, kb * R1:(kb + 1) * R1],
                                                   start=True, stop=True)
                                slgA.inc(ins)
                                if qi >= 9:
                                    t.wait_ge(se1B.h, se1B_b + qi - 8)
                            qb = qi - 8 if qi >= 8 else qi
                            ps_h = psLogB if qi >= 8 else psLogA
                            cb = 1024 if qi >= 8 else 0
                            nf = qb // 4
                            for kb in range(nf):
                                t.matmul(ps_h[:, kb * R1:(kb + 1) * R1], qsl,
                                         K_fm[:, cb + kb * R1: cb + (kb + 1) * R1],
                                         start=True, stop=True)
                            for m in range(qb % 4 + 1):
                                c0 = nf * R1 + m * 128
                                t.matmul(ps_h[:, c0:c0 + 128], qsl,
                                         K_fm[:, cb + c0: cb + c0 + 128],
                                         start=(m == 0), stop=False,
                                         skip_group_check=True)
                            ins = t.matmul(ps_h[:, qb * 128:(qb + 1) * 128],
                                           idn_sb[:], mask_sb[:],
                                           start=False, stop=True,
                                           skip_group_check=True)
                            slg.inc(ins)
                        if j >= 0:
                            t.wait_ge(se1.h, se1_b + j + 1)
                            if j >= 8:
                                t.wait_ge(se1B.h, se1B_b + j - 7)
                            t.wait_ge(se2.h, se2_b + j + 1)
                            for si in range(nsg[j]):
                                g = cum_sg[j] + si
                                b = g % 2
                                if g >= 2:
                                    pk = segs[g - 2]
                                    sem, base = ev_sem(bank_of[pk])
                                    t.wait_ge(sem.h, base + ord_in_bank[pk])
                                for c in range(si * 4, min(j + 1, si * 4 + 4)):
                                    ins = t.matmul(
                                        psT2[b][:, (c % 4) * 128:(c % 4 + 1) * 128],
                                        p2buf[j % 2][:, c * 128:(c + 1) * 128],
                                        diag_li[j % 2][:],
                                        start=True, stop=True,
                                        skip_group_check=True)
                                stx2.inc(ins)
                        if j >= 0:
                            if j >= 2:
                                t.wait_ge(soe.h, soe_b + j - 1)
                            for si in range(nsg[j]):
                                sem, base = ev_sem(bank_of[(j, si)])
                                t.wait_ge(sem.h, base + ord_in_bank[(j, si)])
                                for c in range(si * 4, min(j + 1, si * 4 + 4)):
                                    ins = t.matmul(psO[j % 2][:],
                                                   V_tm[:, c * DK:(c + 1) * DK],
                                                   pT_big[c // 8][:, (c % 8) * 128:(c % 8 + 1) * 128],
                                                   start=(c == 0), stop=(c == j),
                                                   skip_group_check=True)
                            spv.inc(ins)
                    t.drain()
                    t.sem_inc(sbar.h, 1)

                if s == 1:
                    sxn3p_pre_seq[0] = [sxn3p[0].n, sxn3p[1].n]

                    @blk.sync
                    def _(sync):
                        sync.wait_ge(sbar.h, bar_in)
                        for gg in range(2):
                            srcp = xn_scr[0][:, :, gg * R3:(gg + 1) * R3]
                            dstp = xn_r3[gg][:].rearrange("p (c n) -> p c n", c=NCD)
                            sxn3p[gg].inc(sync.dma_start(dstp, srcp), 16)
                        sync.sem_inc(sbar.h, 1)

                @blk.vector
                def _(v):
                    v.wait_ge(sbar.h, bar_in)
                    v.wait_ge(sc0.h, 128)
                    for j in range(NT1):
                        v.wait_ge(se1.h, se1_b + j + 1)
                        if j >= 8:
                            v.wait_ge(se1B.h, se1B_b + j - 7)
                        v.drain()
                        a0 = 16 + 4 * (j % 2)
                        if j >= 8:
                            v.tensor_tensor(stats[:, 22:23], stats[:, a0:a0 + 1],
                                            stats[:, a0 + 1:a0 + 2], ALU.add)
                        else:
                            v.tensor_copy(stats[:, 22:23], stats[:, a0:a0 + 1])
                        v.drain()
                        v.reciprocal(stats[:, 23:24], stats[:, 22:23])
                        if j >= 2:
                            v.wait_ge(stx2.h, stx2_b + cum_sg[j - 1])
                        v.drain()
                        ins = v.tensor_scalar(diag_li[j % 2][:], idn_sb[:],
                                              stats[:, 23:24], None, ALU.mult)
                        se2.inc(ins)
                        for fn in pv.get(j, []):
                            fn(v)
                    v.drain()
                    v.sem_inc(sbar.h, 1)

                @blk.scalar
                def _(sc):
                    sc.wait_ge(sbar.h, bar_in)
                    if s == 1:
                        sc.wait_ge(sc0.h, 128)
                        sc.activation(col(5), th_sb[:], AF.Sin, scale=1.0 / 16)
                        sc.activation(col(4), th_sb[:], AF.Sin, scale=1.0 / 32)
                        sc.drain()
                        sp1.inc(sc.activation(col(6), col(4), AF.Square), 2)
                    for qq in range(NT1 + 1):
                        j = qq - 1
                        if s == 1 and qq == 12:
                            sc.wait_ge(sp2.h, 1)
                            sc.drain()
                            sp3.inc(sc.sign(col(7), col(1)))
                        if s == 1 and qq == 14:
                            sc.wait_ge(sp2.h, 2)
                            sc.drain()
                            sp3.inc(sc.activation(col(6), col(5), AF.Arctan))
                        if qq < NT1:
                            qi = qq
                            if qi >= 2:
                                sc.wait_ge(stx2.h, stx2_b + cum_sg[qi - 1])
                            sc.drain()
                            if qi >= 8:
                                sc.wait_ge(slgA.h, slgA_b + qi - 7)
                                a0 = 16 + 4 * (qi % 2)
                                ins = sc.activation(p2buf[qi % 2][:, 0:1024],
                                                    psLogA[:], AF.Exp,
                                                    bias=fc_sb[:, 1:2],
                                                    accum_out=stats[:, a0:a0 + 1])
                                se1.inc(ins)
                                sc.wait_ge(slg.h, slg_b + qi + 1)
                                sc.drain()
                                ins = sc.activation(
                                    p2buf[qi % 2][:, 1024:(qi + 1) * 128],
                                    psLogB[:, 0:(qi - 7) * 128], AF.Exp,
                                    bias=fc_sb[:, 1:2],
                                    accum_out=stats[:, a0 + 1:a0 + 2])
                                se1B.inc(ins)
                            else:
                                sc.wait_ge(slg.h, slg_b + qi + 1)
                                a0 = 16 + 4 * (qi % 2)
                                ins = sc.activation(p2buf[qi % 2][:, 0:(qi + 1) * 128],
                                                    psLogA[:, 0:(qi + 1) * 128],
                                                    AF.Exp, bias=fc_sb[:, 1:2],
                                                    accum_out=stats[:, a0:a0 + 1])
                                se1.inc(ins)
                        if j >= 0:
                            for si in range(nsg[j]):
                                b = bank_of[(j, si)]
                                sc.wait_ge(stx2.h, stx2_b + cum_sg[j] + si + 1)
                                if j >= 1:
                                    sc.wait_ge(spv.h, spv_b + j)
                                w = segw2(j, si)
                                (st2e if b == 0 else st2eV).inc(sc.copy(
                                    pT_big[si // 2][:, (si % 2) * 512:(si % 2) * 512 + w],
                                    psT2[b][:, 0:w]))
                        j2 = qq - 1
                        if j2 >= 0:
                            sc.wait_ge(spv.h, spv_b + j2 + 1)
                            soe.inc(sc.copy(o_fm2[s][:, j2 * 128:(j2 + 1) * 128],
                                            psO[j2 % 2][:]))
                    sc.drain()
                    sc.sem_inc(sbar.h, 1)

            BAR[0] += 3 + (1 if s == 1 else 0)

          if 3 > _plim:
            raise _Done()

          with ExitStack() as pctx:
                psa = lambda name, shape, dt: pctx.enter_context(
                    nc.psum_tensor(name, shape, dt))
                psW1 = [psa(f"psW1_{i}", [128, R3], F32) for i in range(3)]
                psOut = [psa(f"psOut_{i}", [128, D], F32) for i in range(2)]
                blk = pctx.enter_context(nc.Block())

                NRG = 2 * NR3
                NTJ = 2 * 2 * NR3
                NCHG = NRG * NCF
                sxn3p_b = sxn3p_pre_seq[0]
                sxp_b = [sxp[i].n for i in range(3)]
                sh_b = sh.n
                ssc3_b = ssc3.n
                su2_b = su2.n
                srb_b = srb.n
                sw2_b = sw2.n
                soadd_b = soadd.n
                sost_b = [[sost[h][p].n for p in range(2)] for h in range(2)]
                sxnst_b3 = [sxnst[i].n for i in range(4)]
                bar_in = BAR[0]

                @blk.sync
                def _(sync):
                    sync.wait_ge(sbar.h, bar_in)
                    for i in range(4):
                        sync.wait_ge(sxnst[i].h, sxnst_b3[i])
                    sb2.inc(sync.dma_start(b2q, b2rep), 16)

                    def xn_load(g):
                        srcp = xn_scr[g // NR3][:, :, (g % NR3) * R3:
                                               (g % NR3 + 1) * R3]
                        dstp = xn_r3[g % 2][:].rearrange("p (c n) -> p c n", c=NCD)
                        sxn3p[g % 2].inc(sync.dma_start(dstp, srcp), 16)

                    def x_load(j):
                        sxp[j % 3].inc(sync.dma_start(
                            x_sb[j % 3][:], x_in[j * 128:(j + 1) * 128, :]), 16)

                    x_load(0)
                    x_load(1)
                    x_load(2)
                    for j in range(NTJ):
                        r0 = j * 128
                        for hh in range(2):
                            sync.wait_ge(soadd.h, soadd_b + 2 * j + hh + 1)
                            sost[hh][j % 2].inc(sync.dma_start(
                                out[r0:r0 + 128, hh * 512:(hh + 1) * 512],
                                out_sb2[j % 2][:, hh * 512:(hh + 1) * 512]), 16)
                        if j + 3 < NTJ:
                            x_load(j + 3)
                        if j % 2 == 1:
                            g = (j - 1) // 2
                            if g + 2 < NRG:
                                sync.wait_ge(sh.h, sh_b + NCF * (g + 1))
                                xn_load(g + 2)
                    for hh in range(2):
                        for p in range(2):
                            sync.wait_ge(sost[hh][p].h, sost_b[hh][p] + 16 * NR3 * 2)
                    sync.sem_inc(sbar.h, 1)

                @blk.vector
                def _(v):
                    v.wait_ge(sbar.h, bar_in)
                    for g in range(NRG):
                        for c in range(NCF):
                            ci = NCF * g + c
                            slot = ci % 3
                            v.wait_ge(ssc3.h, ssc3_b + ci + 1)
                            if ci >= 2:
                                v.wait_ge(srb.h, srb_b + ci - 1)
                            su2.inc(v.scalar_tensor_tensor(
                                u2_sb[ci % 2][:], ss_sb[ci % 2][:], Rp_sb[:, c:c + 1],
                                psW1[slot][:], ALU.mult, ALU.add))
                            if c == 16:
                                if g == 0:
                                    v.wait_ge(sb2.h, 16)
                                for tt2 in range(2):
                                    j = 2 * g + tt2
                                    v.wait_ge(sxp[j % 3].h,
                                              sxp_b[j % 3] + 16 * (j // 3 + 1))
                                    for hh in range(2):
                                        if j >= 2:
                                            v.wait_ge(sost[hh][j % 2].h,
                                                      sost_b[hh][j % 2] + 16 * (j // 2))
                                        v.tensor_tensor(
                                            out_sb2[j % 2][:, hh * 512:(hh + 1) * 512],
                                            x_sb[j % 3][:, hh * 512:(hh + 1) * 512],
                                            b2q[:, hh * 512:(hh + 1) * 512], ALU.add)
                        for tt in range(2):
                            j = 2 * g + tt
                            v.wait_ge(sw2.h, sw2_b + NCF * (g + 1))
                            v.drain()
                            for hh in range(2):
                                soadd.inc(v.tensor_tensor(
                                    out_sb2[j % 2][:, hh * 512:(hh + 1) * 512],
                                    psOut[tt][:, hh * 512:(hh + 1) * 512],
                                    out_sb2[j % 2][:, hh * 512:(hh + 1) * 512],
                                    ALU.add))
                    v.drain()
                    v.sem_inc(sbar.h, 1)

                @blk.scalar
                def _(sc):
                    sc.wait_ge(sbar.h, bar_in)
                    sc.wait_ge(sprep.h, 1)
                    for g in range(NRG):
                        for c in range(NCF):
                            ci = NCF * g + c
                            slot = ci % 3
                            sc.wait_ge(sh.h, sh_b + ci + 1)
                            ssc3.inc(sc.activation(ss_sb[ci % 2][:], psW1[slot][:],
                                                   AF.Sin, scale=0.1,
                                                   bias=phb_sb[:, c:c + 1]))
                            cp = ci - 1
                            if cp >= 0:
                                sc.wait_ge(su2.h, su2_b + cp + 1)
                                if cp >= 3:
                                    sc.wait_ge(sw2.h, sw2_b + cp - 2)
                                srb.inc(sc.activation(rb_sb[cp % 3][:], u2_sb[cp % 2][:],
                                                      AF.Relu, bias=b1_sb[:, (cp % NCF):(cp % NCF) + 1]))
                    cp = NCHG - 1
                    sc.wait_ge(su2.h, su2_b + cp + 1)
                    sc.wait_ge(sw2.h, sw2_b + cp - 2)
                    srb.inc(sc.activation(rb_sb[cp % 3][:], u2_sb[cp % 2][:],
                                          AF.Relu, bias=b1_sb[:, NCF - 1:NCF]))
                    sc.drain()
                    sc.sem_inc(sbar.h, 1)

                @blk.tensor
                def _(t):
                    t.wait_ge(sbar.h, bar_in)
                    t.wait_ge(swq.h, 48)
                    t.wait_ge(swr.h, 144)
                    t.wait_ge(sc0.h, 128)
                    t.wait_ge(sprep.h, 1)
                    for ci in range(NCHG + 2):
                        if ci < NCHG:
                            g, c = divmod(ci, NCF)
                            if c == 0:
                                t.wait_ge(sxn3p[g % 2].h,
                                          sxn3p_b[g % 2] + 16 * (g // 2 + 1))
                            slot = ci % 3
                            if ci >= 3:
                                t.wait_ge(su2.h, su2_b + ci - 2)
                            for k in range(NCD):
                                ins = t.matmul(
                                    psW1[slot][:],
                                    W1_sb[:, k * DFF + c * 128: k * DFF + (c + 1) * 128],
                                    xn_r3[g % 2][:, k * R3:(k + 1) * R3],
                                    start=(k == 0), stop=(k == NCD - 1),
                                    skip_group_check=True)
                            sh.inc(ins)
                        cw = ci - 2
                        if cw >= 0 and cw % NCF == 0:
                            gg = cw // NCF
                            for tt in range(2):
                                c0 = (gg % NR3) * R3 + tt * 128
                                for h in range(2):
                                    if gg >= 1:
                                        t.wait_ge(soadd.h,
                                                  soadd_b + 2 * (2 * (gg - 1) + tt) + h + 1)
                                    t.matmul(psOut[tt][:, h * 512:(h + 1) * 512],
                                             o_fm2[gg // NR3][:, c0:c0 + 128],
                                             Wo_sb[:, h * 512:(h + 1) * 512],
                                             start=True, stop=False,
                                             skip_group_check=True)
                        if cw >= 0:
                            cc = cw % NCF
                            t.wait_ge(srb.h, srb_b + cw + 1)
                            for tt in range(2):
                                for h in range(2):
                                    ins = t.matmul(
                                        psOut[tt][:, h * 512:(h + 1) * 512],
                                        rb_sb[cw % 3][:, tt * 128:(tt + 1) * 128],
                                        W2_sb[:, cc * D + h * 512: cc * D + (h + 1) * 512],
                                        start=False, stop=(cc == NCF - 1),
                                        skip_group_check=True)
                            sw2.inc(ins)
                    t.drain()
                    t.sem_inc(sbar.h, 1)

          BAR[0] += 4
        except _Done:
            pass

        with nc.Block() as block:
            @block.sync
            def _(sync):
                sync.wait_ge(sc0.h, 128)

            @block.gpsimd
            def _(g):
                g.wait_ge(swq.h, 48)
                g.wait_ge(swr.h, 144)

    return nc


def kernel(L0_attn_x, L0_attn_Wq, L0_attn_Wk, L0_attn_Wv, L0_attn_Wo,
           L0_ffn_W1, L0_ffn_b1, L0_ffn_quantum_phases,
           L0_ffn_quantum_weights_real, L0_ffn_quantum_weights_imag,
           L0_ffn_W2, L0_ffn_b2):
    x = np.ascontiguousarray(np.asarray(L0_attn_x, dtype=np.float32))
    wq = np.ascontiguousarray(np.asarray(L0_attn_Wq, dtype=np.float32))
    wk = np.ascontiguousarray(np.asarray(L0_attn_Wk, dtype=np.float32))
    wv = np.ascontiguousarray(np.asarray(L0_attn_Wv, dtype=np.float32))
    wo = np.ascontiguousarray(np.asarray(L0_attn_Wo, dtype=np.float32))
    w1 = np.ascontiguousarray(np.asarray(L0_ffn_W1, dtype=np.float32))
    w2 = np.ascontiguousarray(np.asarray(L0_ffn_W2, dtype=np.float32))
    b1 = np.asarray(L0_ffn_b1, dtype=np.float32)
    th = np.asarray(L0_ffn_quantum_phases, dtype=np.float32)
    wr = np.asarray(L0_ffn_quantum_weights_real, dtype=np.float32)
    wi = np.asarray(L0_ffn_quantum_weights_imag, dtype=np.float32)
    b2 = np.asarray(L0_ffn_b2, dtype=np.float32)

    per_feat = lambda a: np.ascontiguousarray(a.reshape(NCF, 128).T)
    b1t = per_feat(b1)
    tht = per_feat(th)
    wrt = per_feat(wr)
    wit = per_feat(wi)
    b2rep = np.ascontiguousarray(np.broadcast_to(b2.reshape(1, D),
                                                 (128, D))).astype(np.float32)
    maskd = np.where(np.triu(np.ones((128, 128), dtype=bool), k=1),
                     np.float32(-1e9), np.float32(0.0)).astype(ml_dtypes.bfloat16)
    idn = np.eye(128, dtype=ml_dtypes.bfloat16)
    ones1 = np.ones((1, 128), dtype=ml_dtypes.bfloat16)
    fcst = np.zeros((128, 8), dtype=np.float32)
    fcst[:, 0] = LNEPS
    fcst[:, 1] = -BS

    nc = build_nc()
    in_maps = []
    for i in range(NCORE):
        xc = np.ascontiguousarray(x[NSEQ * i:NSEQ * (i + 1)].reshape(NTOK, D))
        in_maps.append({
            "x": xc, "w1": w1, "w2": w2, "wq": wq, "wk": wk, "wv": wv, "wo": wo,
            "b1t": b1t, "tht": tht, "wrt": wrt, "wit": wit, "b2rep": b2rep,
            "maskd": maskd, "idn": idn, "ones1": ones1, "fcst": fcst,
        })
    res = run_bass_kernel_spmd(nc, in_maps, core_ids=list(range(NCORE)))
    outs = [r["out"].reshape(NSEQ, T, D) for r in res.results]
    return np.ascontiguousarray(np.concatenate(outs, axis=0))


# version 37
# speedup vs baseline: 1.0151x; 1.0151x over previous
import os
import numpy as np
import ml_dtypes
from contextlib import ExitStack

import concourse.bass as bass
import concourse.mybir as mybir
from concourse.bass_utils import run_bass_kernel_spmd

F32 = mybir.dt.float32
BF16 = mybir.dt.bfloat16
AF = mybir.ActivationFunctionType
ALU = mybir.AluOpType
AX = mybir.AxisListType

NCORE = 8
T = 2048
D = 1024
DK = 128
DFF = 4096
NSEQ = 2
NTOK = NSEQ * T
NCD = D // 128
NCF = DFF // 128
NT1 = T // 128
R1 = 512
NR1 = T // R1
R3 = 256
NR3 = T // R3
QSCALE = float(DK) ** -0.5
BS = 12.0
LNEPS = 1e-5


class Ctr:

    def __init__(self, h):
        self.h = h
        self.n = 0

    def inc(self, ins, k=1):
        ins.then_inc(self.h, k)
        self.n += k
        return self.n


def build_nc():
    nc = bass.Bass("TRN2", target_bir_lowering=False)

    x_in = nc.dram_tensor("x", [NTOK, D], F32, kind="ExternalInput").ap()
    w1 = nc.dram_tensor("w1", [D, DFF], F32, kind="ExternalInput").ap()
    w2 = nc.dram_tensor("w2", [DFF, D], F32, kind="ExternalInput").ap()
    wq = nc.dram_tensor("wq", [D, DK], F32, kind="ExternalInput").ap()
    wk = nc.dram_tensor("wk", [D, DK], F32, kind="ExternalInput").ap()
    wv = nc.dram_tensor("wv", [D, DK], F32, kind="ExternalInput").ap()
    wo = nc.dram_tensor("wo", [DK, D], F32, kind="ExternalInput").ap()
    b1t = nc.dram_tensor("b1t", [128, NCF], F32, kind="ExternalInput").ap()
    tht = nc.dram_tensor("tht", [128, NCF], F32, kind="ExternalInput").ap()
    wrt = nc.dram_tensor("wrt", [128, NCF], F32, kind="ExternalInput").ap()
    wit = nc.dram_tensor("wit", [128, NCF], F32, kind="ExternalInput").ap()
    b2rep = nc.dram_tensor("b2rep", [128, D], F32, kind="ExternalInput").ap()
    maskd = nc.dram_tensor("maskd", [128, 128], BF16, kind="ExternalInput").ap()
    idn = nc.dram_tensor("idn", [128, 128], BF16, kind="ExternalInput").ap()
    ones1 = nc.dram_tensor("ones1", [1, 128], BF16, kind="ExternalInput").ap()
    fcst = nc.dram_tensor("fcst", [128, 8], F32, kind="ExternalInput").ap()
    out = nc.dram_tensor("out", [NTOK, D], F32, kind="ExternalOutput").ap()
    _xnkind = "ExternalOutput" if os.environ.get("KDBG_XN") else "Internal"
    xn_scr = [nc.dram_tensor(f"xn_scr{i}", [128, NCD, T], BF16, kind=_xnkind).ap()
              for i in range(NSEQ)]

    with ExitStack() as ctx:
        sbuf = lambda name, shape, dt: ctx.enter_context(nc.sbuf_tensor(name, shape, dt))
        sem = lambda name: Ctr(ctx.enter_context(nc.semaphore(name)))

        W1_sb = sbuf("W1_sb", [128, NCD * DFF], BF16)
        W2_sb = sbuf("W2_sb", [128, NCF * D], BF16)
        Wq_sb = sbuf("Wq_sb", [128, NCD * DK], BF16)
        Wk_sb = sbuf("Wk_sb", [128, NCD * DK], BF16)
        Wv_sb = sbuf("Wv_sb", [128, NCD * DK], BF16)
        Wo_sb = sbuf("Wo_sb", [128, D], BF16)
        b1_sb = sbuf("b1_sb", [128, NCF], F32)
        th_sb = sbuf("th_sb", [128, NCF], F32)
        wr_sb = sbuf("wr_sb", [128, NCF], F32)
        wi_sb = sbuf("wi_sb", [128, NCF], F32)
        Rp_sb = sbuf("Rp_sb", [128, NCF], F32)
        phb_sb = sbuf("phb_sb", [128, NCF], F32)
        prep = sbuf("prep", [128, 8 * NCF], F32)
        mask_sb = sbuf("mask_sb", [128, 128], BF16)
        idn_sb = sbuf("idn_sb", [128, 128], BF16)
        ones_sb = sbuf("ones_sb", [1, 128], BF16)
        diag_li = [sbuf(f"diag_li{i}", [128, 128], BF16) for i in range(2)]
        fc_sb = sbuf("fc_sb", [128, 8], F32)
        stats = sbuf("stats", [128, 64], F32)

        x_sb = [sbuf(f"x_sb{i}", [128, D], F32) for i in range(3)]

        xn_rg = sbuf("xn_rg", [128, NCD * R1], BF16)
        Q_fm = sbuf("Q_fm", [128, T], BF16)
        b2q = Q_fm[:].bitcast(F32)
        K_fm = sbuf("K_fm", [128, T], BF16)
        V_tm = sbuf("V_tm", [128, NT1 * DK], BF16)
        o_fm2 = [sbuf(f"o_fm{i}", [128, T], BF16) for i in range(2)]
        probs = sbuf("probs", [128, T], BF16)
        xn_tm = [probs[:, 0:D], probs[:, D:2 * D], None]
        pT_big = [sbuf(f"pT_big{i}", [128, 1024], BF16) for i in range(2)]
        xn_tm[2] = pT_big[1][:]
        xn_r3 = [sbuf(f"xn_r3{i}", [128, NCD * R3], BF16) for i in range(2)]
        x_sb.append(xn_r3[0][:].bitcast(F32))
        x_sb.append(xn_r3[1][:].bitcast(F32))
        rb_sb = [sbuf(f"rb_sb{i}", [128, R3], BF16) for i in range(3)]
        ss_sb = [sbuf(f"ss_sb{i}", [128, R3], BF16) for i in range(2)]
        u2_sb = [sbuf(f"u2_sb{i}", [128, R3], F32) for i in range(2)]
        out_sb = sbuf("out_sb", [128, D], F32)
        out_sb2 = [out_sb, probs[:].bitcast(F32)]

        sc0 = sem("sc0")
        swq = sem("swq")
        swr = sem("swr")
        sbar = sem("sbar")
        sxp = [sem(f"sxp{i}") for i in range(5)]
        ssq = sem("ssq")
        svar = sem("svar")
        srst = sem("srst")
        sln = sem("sln")
        sdg3 = sem("sdg3")
        stx = sem("stx")
        stxe = sem("stxe")
        sxnst = [sem(f"sxnst{i}") for i in range(4)]
        sqk = sem("sqk")
        sqke = sem("sqke")
        slg = sem("slg")
        smd = sem("smd")
        se1 = sem("se1")
        se1B = sem("se1B")
        se2 = sem("se2")
        stx2 = sem("stx2")
        slgA = sem("slgA")
        st2e = sem("st2e")
        st2eV = sem("st2eV")
        spv = sem("spv")
        soe = sem("soe")
        sxn3p = [sem("sxn3p0"), sem("sxn3p1")]
        sh = sem("sh")
        ssc3 = sem("ssc3")
        su2 = sem("su2")
        srb = sem("srb")
        sw2 = sem("sw2")
        soadd = sem("soadd")
        sost = [[sem(f"sost{h}{p}") for p in range(2)] for h in range(2)]
        sp1 = sem("sp1")
        sp2 = sem("sp2")
        sp3 = sem("sp3")
        sb2 = sem("sb2")
        sprep = sem("sprep")

        BAR = [0]
        _plim = int(os.environ.get("KDBG_PHASES", "99"))

        class _Done(Exception):
            pass

        with nc.Block(no_gpsimd_drain=True) as block:
            @block.sync
            def _(sync):
                for dst, src in [(fc_sb, fcst), (idn_sb, idn), (mask_sb, maskd),
                                 (ones_sb, ones1), (b1_sb, b1t), (th_sb, tht),
                                 (wr_sb, wrt), (wi_sb, wit)]:
                    sc0.inc(sync.dma_start(dst[:], src), 16)

            @block.gpsimd
            def _(g):
                for w_hbm, w_sb in [(wq, Wq_sb), (wk, Wk_sb), (wv, Wv_sb)]:
                    src = w_hbm.rearrange("(c p) k -> p c k", c=NCD, p=128)
                    dst = w_sb[:].rearrange("p (c k) -> p c k", c=NCD)
                    swq.inc(g.dma_start(dst, src), 16)
                swr.inc(g.dma_start(Wo_sb[:], wo), 16)


        try:
          for s in range(NSEQ):
            xb = s * T
            if 3 * s + 1 > _plim:
                raise _Done()
            with ExitStack() as pctx:
                psa = lambda name, shape, dt: pctx.enter_context(
                    nc.psum_tensor(name, shape, dt))
                psTxB = [psa(f"psTx{s}_{i}", [128, 1024], F32) for i in range(2)]
                psQ = psa(f"psQ{s}", [128, R1], F32)
                psK = psa(f"psK{s}", [128, R1], F32)
                psVB = psa(f"psVB{s}", [128, R1], F32)
                blk = pctx.enter_context(nc.Block(no_gpsimd_drain=True))

                sxp_b = [sxp[i].n for i in range(5)]
                ssq_b = ssq.n
                svar_b = svar.n
                srst_b = srst.n
                sln_b = sln.n
                sdg3_b = sdg3.n
                stx_b = stx.n
                stxe_b = stxe.n
                sxnst_b = [sxnst[i].n for i in range(4)]
                sqk_b = sqk.n
                sqke_b = sqke.n
                if s == 0:
                    sxn3p_pre_seq = {}
                bar_in = BAR[0]

                @blk.sync
                def _(sync):
                    if bar_in:
                        sync.wait_ge(sbar.h, bar_in)
                    for j in range(NT1):
                        if j >= 5:
                            sync.wait_ge(sln.h, sln_b + j - 4)
                            sync.wait_ge(svar.h, svar_b + j - 4)
                        sxp[j % 5].inc(sync.dma_start(
                            x_sb[j % 5][:], x_in[xb + j * 128: xb + (j + 1) * 128, :]), 16)
                    for pp in range(5):
                        sync.wait_ge(sxp[pp].h, sxp_b[pp] + 16 * len(
                            [j for j in range(NT1) if j % 5 == pp]))
                    sync.sem_inc(sbar.h, 1)

                @blk.gpsimd
                def _(g):
                    if bar_in:
                        g.wait_ge(sbar.h, bar_in)
                    for j in range(NT1):
                        g.wait_ge(stxe.h, stxe_b + j + 1)
                        src = xn_rg[:, (j % 4) * 1024:(j % 4 + 1) * 1024].rearrange(
                            "p (c n) -> p c n", c=NCD)
                        dst = xn_scr[s][:, :, j * 128:(j + 1) * 128]
                        sxnst[j % 4].inc(g.dma_start(dst, src), 16)
                    if s == 0:
                        for jw in range(4):
                            srcw = w1.rearrange("(a c p) f -> a p c f",
                                                a=4, c=2, p=128)[jw]
                            dstw = W1_sb[:, jw * 2 * DFF:(jw + 1) * 2 * DFF].rearrange(
                                "p (c f) -> p c f", c=2)
                            swr.inc(g.dma_start(dstw, srcw), 16)
                    else:
                        for jw in range(4):
                            srcw = w2.rearrange("(a c p) d -> a p c d",
                                                a=4, c=8, p=128)[jw]
                            dstw = W2_sb[:, jw * 8 * D:(jw + 1) * 8 * D].rearrange(
                                "p (c d) -> p c d", c=8)
                            swr.inc(g.dma_start(dstw, srcw), 16)
                    for i in range(4):
                        g.wait_ge(sxnst[i].h, sxnst_b[i] + 16 * NR1)
                    g.sem_inc(sbar.h, 1)

                NF = NCF
                col = lambda i: prep[:, i * NF:(i + 1) * NF]
                pv = {}
                if s == 0:
                    vops = [lambda v: v.wait_ge(sp1.h, 2)]

                    def _o(fn):
                        vops.append(lambda v, fn=fn: (v.drain(), fn(v)))
                    _o(lambda v: v.tensor_scalar(col(1), col(6), -2.0, 1.0,
                                                 ALU.mult, ALU.add))
                    _o(lambda v: v.tensor_copy(col(0), col(5)))
                    for _ in range(4):
                        _o(lambda v: v.scalar_tensor_tensor(col(2), col(0), 2.0,
                                                            col(1), ALU.mult, ALU.mult))
                        _o(lambda v: v.tensor_tensor(col(3), col(0), col(0), ALU.mult))
                        _o(lambda v: v.tensor_scalar(col(1), col(3), -2.0, 1.0,
                                                     ALU.mult, ALU.add))
                        _o(lambda v: v.tensor_copy(col(0), col(2)))
                    _o(lambda v: v.tensor_tensor(col(2), col(1), wr_sb[:], ALU.mult))
                    _o(lambda v: v.tensor_tensor(col(3), col(0), wi_sb[:], ALU.mult))
                    _o(lambda v: v.tensor_tensor(col(2), col(2), col(3), ALU.add))
                    _o(lambda v: v.tensor_tensor(col(3), col(2), wi_sb[:], ALU.mult))
                    _o(lambda v: v.tensor_scalar(col(4), col(1), -1.0, None, ALU.mult))
                    _o(lambda v: v.tensor_tensor(col(4), col(1), col(4), ALU.max))
                    _o(lambda v: v.tensor_scalar(col(4), col(4), 1.0, None, ALU.add))
                    _o(lambda v: v.reciprocal(col(5), col(4)))
                    _o(lambda v: sp2.inc(v.tensor_tensor(col(4), col(0), col(5),
                                                         ALU.mult)))
                    for i, fn in enumerate(vops):
                        pv.setdefault(min(i // 3, 9), []).append(fn)

                    def _mid(v):
                        v.wait_ge(sp3.h, 1)
                        v.drain()
                        sp2.inc(v.tensor_tensor(col(5), col(4), col(7), ALU.mult))
                    pv.setdefault(13, []).append(_mid)

                    def _tail(v):
                        v.wait_ge(sp3.h, 2)
                        v.drain()
                        v.scalar_tensor_tensor(Rp_sb[:], col(3), 0.1, col(7),
                                               ALU.mult, ALU.mult)
                        v.drain()
                        v.tensor_scalar(col(5), col(6), 2.0, None, ALU.mult)
                        v.drain()
                        ins = v.scalar_tensor_tensor(phb_sb[:], b1_sb[:], 0.1,
                                                     col(5), ALU.mult, ALU.add)
                        sprep.inc(ins)
                    pv.setdefault(15, []).append(_tail)

                @blk.vector
                def _(v):
                    if bar_in:
                        v.wait_ge(sbar.h, bar_in)
                    for j in range(NT1 + 2):
                        if j < NT1:
                            p2 = 16 * (j % 4)
                            v.wait_ge(sxp[j % 5].h, sxp_b[j % 5] + 16 * (j // 5 + 1))
                            v.bn_stats(stats[:, p2 + 0:p2 + 6], x_sb[j % 5][:, 0:512])
                            v.bn_stats(stats[:, p2 + 6:p2 + 12], x_sb[j % 5][:, 512:1024])
                            v.drain()
                            ins = v.bn_aggr(stats[:, p2 + 12:p2 + 14],
                                            stats[:, p2 + 0:p2 + 12])
                            svar.inc(ins)
                        jj = j - 2
                        if jj >= 0:
                            p2 = 16 * (jj % 4)
                            v.wait_ge(srst.h, srst_b + jj + 1)
                            v.reciprocal(stats[:, p2 + 14:p2 + 15],
                                         stats[:, p2 + 15:p2 + 16])
                            if jj >= 2:
                                v.wait_ge(stx.h, stx_b + jj - 1)
                            v.drain()
                            ins = v.tensor_scalar(
                                diag_li[jj % 2][:], idn_sb[:],
                                stats[:, p2 + 14:p2 + 15], -1.0,
                                ALU.mult, ALU.mult)
                            sdg3.inc(ins)
                        if j < NT1:
                            for fn in pv.get(j, []):
                                fn(v)
                    v.drain()
                    v.sem_inc(sbar.h, 1)

                @blk.scalar
                def _(sc):
                    if bar_in:
                        sc.wait_ge(sbar.h, bar_in)
                    sc.wait_ge(sc0.h, 128)
                    if s == 0:
                        sc.activation(col(5), th_sb[:], AF.Sin, scale=1.0 / 16)
                        sc.activation(col(4), th_sb[:], AF.Sin, scale=1.0 / 32)
                        sc.drain()
                        sp1.inc(sc.activation(col(6), col(4), AF.Square), 2)
                    for j in range(NT1 + 2):
                        jj = j - 1
                        if s == 0 and j == 12:
                            sc.wait_ge(sp2.h, 1)
                            sc.drain()
                            sp3.inc(sc.sign(col(7), col(1)))
                        if s == 0 and j == 14:
                            sc.wait_ge(sp2.h, 2)
                            sc.drain()
                            sp3.inc(sc.activation(col(6), col(5), AF.Arctan))
                        if 0 <= jj < NT1:
                            p2 = 16 * (jj % 4)
                            sc.wait_ge(svar.h, svar_b + jj + 1)
                            srst.inc(sc.activation(stats[:, p2 + 15:p2 + 16],
                                                   stats[:, p2 + 13:p2 + 14],
                                                   AF.Sqrt, bias=fc_sb[:, 0:1]))
                            if jj >= 3:
                                sc.wait_ge(stx.h, stx_b + jj - 2)
                            ins = sc.activation(xn_tm[jj % 3], x_sb[jj % 5][:],
                                                AF.Identity, scale=-1.0,
                                                bias=stats[:, p2 + 12:p2 + 13])
                            sln.inc(ins)
                        je = j - 2
                        if 0 <= je < NT1:
                            r, tt = je // 4, je % 4
                            if tt == 0 and r > 0:
                                sc.wait_ge(sqk.h, sqk_b + 4 * r)
                            if je >= 4:
                                sc.wait_ge(sxnst[je % 4].h,
                                           sxnst_b[je % 4] + 16 * (je // 4))
                            sc.wait_ge(stx.h, stx_b + je + 1)
                            stxe.inc(sc.copy(xn_rg[:, tt * 1024:(tt + 1) * 1024],
                                             psTxB[je % 2][:]))
                            if tt == 0 and r >= 1:
                                rp = r - 1
                                sc.wait_ge(sqk.h, sqk_b + 4 * (rp + 1))
                                sqke.inc(sc.activation(Q_fm[:, rp * R1:(rp + 1) * R1],
                                                       psQ[:], AF.Copy, scale=QSCALE))
                                sqke.inc(sc.copy(K_fm[:, rp * R1:(rp + 1) * R1], psK[:]))
                                sqke.inc(sc.copy(V_tm[:, rp * R1:(rp + 1) * R1], psVB[:]))

                    rp = NR1 - 1
                    sc.wait_ge(sqk.h, sqk_b + 4 * (rp + 1))
                    sqke.inc(sc.activation(Q_fm[:, rp * R1:(rp + 1) * R1],
                                           psQ[:], AF.Copy, scale=QSCALE))
                    sqke.inc(sc.copy(K_fm[:, rp * R1:(rp + 1) * R1], psK[:]))
                    sqke.inc(sc.copy(V_tm[:, rp * R1:(rp + 1) * R1], psVB[:]))
                    sc.drain()
                    sc.sem_inc(sbar.h, 1)

                @blk.tensor
                def _(t):
                    if bar_in:
                        t.wait_ge(sbar.h, bar_in)
                    t.wait_ge(sc0.h, 128)
                    if s == 0:
                        t.wait_ge(swq.h, 48)
                    def qkv_slice(jq):
                        tt, r = jq % 4, jq // 4
                        if tt == 0 and r > 0:
                            t.wait_ge(sqke.h, sqke_b + 3 * r)
                        t.wait_ge(stxe.h, stxe_b + jq + 1)
                        for ci in range(NCD):
                            t.matmul(psQ[:, tt * 128:(tt + 1) * 128],
                                     Wq_sb[:, ci * DK:(ci + 1) * DK],
                                     xn_rg[:, tt * 1024 + ci * 128:
                                            tt * 1024 + (ci + 1) * 128],
                                     start=(ci == 0), stop=(ci == NCD - 1),
                                     skip_group_check=True)
                        for ci in range(NCD):
                            t.matmul(psK[:, tt * 128:(tt + 1) * 128],
                                     Wk_sb[:, ci * DK:(ci + 1) * DK],
                                     xn_rg[:, tt * 1024 + ci * 128:
                                            tt * 1024 + (ci + 1) * 128],
                                     start=(ci == 0), stop=(ci == NCD - 1),
                                     skip_group_check=True)
                        for ci in range(NCD):
                            ins = t.matmul(psVB[:, tt * 128:(tt + 1) * 128],
                                           xn_rg[:, tt * 1024 + ci * 128:
                                                  tt * 1024 + (ci + 1) * 128],
                                           Wv_sb[:, ci * DK:(ci + 1) * DK],
                                           start=(ci == 0), stop=(ci == NCD - 1),
                                           skip_group_check=True)
                        sqk.inc(ins)

                    for j in range(NT1):
                        t.wait_ge(sln.h, sln_b + j + 1)
                        t.wait_ge(sdg3.h, sdg3_b + j + 1)
                        if j >= 2:
                            t.wait_ge(stxe.h, stxe_b + j - 1)
                        for c in range(NCD):
                            ins = t.matmul(psTxB[j % 2][:, c * 128:(c + 1) * 128],
                                           xn_tm[j % 3][:, c * 128:(c + 1) * 128],
                                           diag_li[j % 2][:],
                                           start=True, stop=True,
                                           skip_group_check=True)
                        stx.inc(ins)
                        if j >= 1:
                            qkv_slice(j - 1)
                    qkv_slice(NT1 - 1)
                    t.drain()
                    t.sem_inc(sbar.h, 1)

            BAR[0] += 5
            if 3 * s + 2 > _plim:
                raise _Done()

            with ExitStack() as pctx:
                psa = lambda name, shape, dt: pctx.enter_context(
                    nc.psum_tensor(name, shape, dt))
                psLogA = psa(f"psLogA{s}", [128, 1024], F32)
                psLogB = psa(f"psLogB{s}", [128, 1024], F32)
                psT2 = [psa(f"psT2{s}_{i}", [128, 512], F32) for i in range(2)]
                psO = [psa(f"psO{s}_{i}", [128, 128], F32) for i in range(2)]
                blk = pctx.enter_context(nc.Block())

                slg_b = slg.n
                se1_b = se1.n
                se1B_b = se1B.n
                se2_b = se2.n
                stx2_b = stx2.n
                st2e_b = st2e.n
                st2eV_b = st2eV.n
                spv_b = spv.n
                soe_b = soe.n
                bar_in = BAR[0]
                cum = [0]
                for qi in range(NT1):
                    cum.append(cum[-1] + qi + 1)

                slgA_b = slgA.n
                p2buf = [xn_rg[:, 0:T], xn_rg[:, T:2 * T]]
                nsg = [j // 4 + 1 for j in range(NT1)]
                cum_sg = [0]
                for j in range(NT1):
                    cum_sg.append(cum_sg[-1] + nsg[j])
                segs = [(j, si) for j in range(NT1) for si in range(nsg[j])]
                bank_of, ord_in_bank = {}, {}
                _cnt = [0, 0]
                for g, key in enumerate(segs):
                    b = g % 2
                    bank_of[key] = b
                    _cnt[b] += 1
                    ord_in_bank[key] = _cnt[b]
                segw2 = lambda j, si: (min(j + 1 - si * 4, 4)) * 128
                ev_sem = lambda b: (st2e, st2e_b) if b == 0 else (st2eV, st2eV_b)

                @blk.tensor
                def _(t):
                    t.wait_ge(sbar.h, bar_in)
                    for qq in range(NT1 + 1):
                        j = qq - 1
                        if qq < NT1:
                            qi = qq
                            if qi >= 1:
                                t.wait_ge(se1.h, se1_b + qi)
                            qsl = Q_fm[:, qi * 128:(qi + 1) * 128]
                            if qi >= 8:
                                for kb in range(2):
                                    ins = t.matmul(psLogA[:, kb * R1:(kb + 1) * R1],
                                                   qsl, K_fm[:, kb * R1:(kb + 1) * R1],
                                                   start=True, stop=True)
                                slgA.inc(ins)
                                if qi >= 9:
                                    t.wait_ge(se1B.h, se1B_b + qi - 8)
                            qb = qi - 8 if qi >= 8 else qi
                            ps_h = psLogB if qi >= 8 else psLogA
                            cb = 1024 if qi >= 8 else 0
                            nf = qb // 4
                            for kb in range(nf):
                                t.matmul(ps_h[:, kb * R1:(kb + 1) * R1], qsl,
                                         K_fm[:, cb + kb * R1: cb + (kb + 1) * R1],
                                         start=True, stop=True)
                            for m in range(qb % 4 + 1):
                                c0 = nf * R1 + m * 128
                                t.matmul(ps_h[:, c0:c0 + 128], qsl,
                                         K_fm[:, cb + c0: cb + c0 + 128],
                                         start=(m == 0), stop=False,
                                         skip_group_check=True)
                            ins = t.matmul(ps_h[:, qb * 128:(qb + 1) * 128],
                                           idn_sb[:], mask_sb[:],
                                           start=False, stop=True,
                                           skip_group_check=True)
                            slg.inc(ins)
                        if j >= 0:
                            t.wait_ge(se1.h, se1_b + j + 1)
                            if j >= 8:
                                t.wait_ge(se1B.h, se1B_b + j - 7)
                            t.wait_ge(se2.h, se2_b + j + 1)
                            for si in range(nsg[j]):
                                g = cum_sg[j] + si
                                b = g % 2
                                if g >= 2:
                                    pk = segs[g - 2]
                                    sem, base = ev_sem(bank_of[pk])
                                    t.wait_ge(sem.h, base + ord_in_bank[pk])
                                for c in range(si * 4, min(j + 1, si * 4 + 4)):
                                    ins = t.matmul(
                                        psT2[b][:, (c % 4) * 128:(c % 4 + 1) * 128],
                                        p2buf[j % 2][:, c * 128:(c + 1) * 128],
                                        diag_li[j % 2][:],
                                        start=True, stop=True,
                                        skip_group_check=True)
                                stx2.inc(ins)
                        if j >= 0:
                            if j >= 2:
                                t.wait_ge(soe.h, soe_b + j - 1)
                            for si in range(nsg[j]):
                                sem, base = ev_sem(bank_of[(j, si)])
                                t.wait_ge(sem.h, base + ord_in_bank[(j, si)])
                                for c in range(si * 4, min(j + 1, si * 4 + 4)):
                                    ins = t.matmul(psO[j % 2][:],
                                                   V_tm[:, c * DK:(c + 1) * DK],
                                                   pT_big[c // 8][:, (c % 8) * 128:(c % 8 + 1) * 128],
                                                   start=(c == 0), stop=(c == j),
                                                   skip_group_check=True)
                            spv.inc(ins)
                    t.drain()
                    t.sem_inc(sbar.h, 1)

                if s == 1:
                    sxn3p_pre_seq[0] = [sxn3p[0].n, sxn3p[1].n]

                    @blk.sync
                    def _(sync):
                        sync.wait_ge(sbar.h, bar_in)
                        for gg in range(2):
                            srcp = xn_scr[0][:, :, gg * R3:(gg + 1) * R3]
                            dstp = xn_r3[gg][:].rearrange("p (c n) -> p c n", c=NCD)
                            sxn3p[gg].inc(sync.dma_start(dstp, srcp), 16)
                        sync.sem_inc(sbar.h, 1)

                @blk.vector
                def _(v):
                    v.wait_ge(sbar.h, bar_in)
                    v.wait_ge(sc0.h, 128)
                    for j in range(NT1):
                        v.wait_ge(se1.h, se1_b + j + 1)
                        if j >= 8:
                            v.wait_ge(se1B.h, se1B_b + j - 7)
                        v.drain()
                        a0 = 16 + 4 * (j % 2)
                        if j >= 8:
                            v.tensor_tensor(stats[:, 22:23], stats[:, a0:a0 + 1],
                                            stats[:, a0 + 1:a0 + 2], ALU.add)
                        else:
                            v.tensor_copy(stats[:, 22:23], stats[:, a0:a0 + 1])
                        v.drain()
                        v.reciprocal(stats[:, 23:24], stats[:, 22:23])
                        if j >= 2:
                            v.wait_ge(stx2.h, stx2_b + cum_sg[j - 1])
                        v.drain()
                        ins = v.tensor_scalar(diag_li[j % 2][:], idn_sb[:],
                                              stats[:, 23:24], None, ALU.mult)
                        se2.inc(ins)
                        for si in range(nsg[j]):
                            b = bank_of[(j, si)]
                            v.wait_ge(stx2.h, stx2_b + cum_sg[j] + si + 1)
                            if j >= 1:
                                v.wait_ge(spv.h, spv_b + j)
                            v.drain()
                            w = segw2(j, si)
                            (st2e if b == 0 else st2eV).inc(v.tensor_copy(
                                pT_big[si // 2][:, (si % 2) * 512:(si % 2) * 512 + w],
                                psT2[b][:, 0:w]))
                    v.drain()
                    v.sem_inc(sbar.h, 1)

                @blk.scalar
                def _(sc):
                    sc.wait_ge(sbar.h, bar_in)
                    for qq in range(NT1 + 1):
                        j = qq - 1
                        if qq < NT1:
                            qi = qq
                            if qi >= 2:
                                sc.wait_ge(stx2.h, stx2_b + cum_sg[qi - 1])
                            sc.drain()
                            if qi >= 8:
                                sc.wait_ge(slgA.h, slgA_b + qi - 7)
                                a0 = 16 + 4 * (qi % 2)
                                ins = sc.activation(p2buf[qi % 2][:, 0:1024],
                                                    psLogA[:], AF.Exp,
                                                    bias=fc_sb[:, 1:2],
                                                    accum_out=stats[:, a0:a0 + 1])
                                se1.inc(ins)
                                sc.wait_ge(slg.h, slg_b + qi + 1)
                                sc.drain()
                                ins = sc.activation(
                                    p2buf[qi % 2][:, 1024:(qi + 1) * 128],
                                    psLogB[:, 0:(qi - 7) * 128], AF.Exp,
                                    bias=fc_sb[:, 1:2],
                                    accum_out=stats[:, a0 + 1:a0 + 2])
                                se1B.inc(ins)
                            else:
                                sc.wait_ge(slg.h, slg_b + qi + 1)
                                a0 = 16 + 4 * (qi % 2)
                                ins = sc.activation(p2buf[qi % 2][:, 0:(qi + 1) * 128],
                                                    psLogA[:, 0:(qi + 1) * 128],
                                                    AF.Exp, bias=fc_sb[:, 1:2],
                                                    accum_out=stats[:, a0:a0 + 1])
                                se1.inc(ins)
                        j2 = qq - 1
                        if j2 >= 0:
                            sc.wait_ge(spv.h, spv_b + j2 + 1)
                            soe.inc(sc.copy(o_fm2[s][:, j2 * 128:(j2 + 1) * 128],
                                            psO[j2 % 2][:]))
                    sc.drain()
                    sc.sem_inc(sbar.h, 1)

            BAR[0] += 3 + (1 if s == 1 else 0)

          if 3 > _plim:
            raise _Done()

          with ExitStack() as pctx:
                psa = lambda name, shape, dt: pctx.enter_context(
                    nc.psum_tensor(name, shape, dt))
                psW1 = [psa(f"psW1_{i}", [128, R3], F32) for i in range(3)]
                psOut = [psa(f"psOut_{i}", [128, D], F32) for i in range(2)]
                blk = pctx.enter_context(nc.Block())

                NRG = 2 * NR3
                NTJ = 2 * 2 * NR3
                NCHG = NRG * NCF
                sxn3p_b = sxn3p_pre_seq[0]
                sxp_b = [sxp[i].n for i in range(3)]
                sh_b = sh.n
                ssc3_b = ssc3.n
                su2_b = su2.n
                srb_b = srb.n
                sw2_b = sw2.n
                soadd_b = soadd.n
                sost_b = [[sost[h][p].n for p in range(2)] for h in range(2)]
                sxnst_b3 = [sxnst[i].n for i in range(4)]
                bar_in = BAR[0]

                @blk.sync
                def _(sync):
                    sync.wait_ge(sbar.h, bar_in)
                    for i in range(4):
                        sync.wait_ge(sxnst[i].h, sxnst_b3[i])
                    sb2.inc(sync.dma_start(b2q, b2rep), 16)

                    def xn_load(g):
                        srcp = xn_scr[g // NR3][:, :, (g % NR3) * R3:
                                               (g % NR3 + 1) * R3]
                        dstp = xn_r3[g % 2][:].rearrange("p (c n) -> p c n", c=NCD)
                        sxn3p[g % 2].inc(sync.dma_start(dstp, srcp), 16)

                    def x_load(j):
                        sxp[j % 3].inc(sync.dma_start(
                            x_sb[j % 3][:], x_in[j * 128:(j + 1) * 128, :]), 16)

                    x_load(0)
                    x_load(1)
                    x_load(2)
                    for j in range(NTJ):
                        r0 = j * 128
                        for hh in range(2):
                            sync.wait_ge(soadd.h, soadd_b + 2 * j + hh + 1)
                            sost[hh][j % 2].inc(sync.dma_start(
                                out[r0:r0 + 128, hh * 512:(hh + 1) * 512],
                                out_sb2[j % 2][:, hh * 512:(hh + 1) * 512]), 16)
                        if j + 3 < NTJ:
                            x_load(j + 3)
                        if j % 2 == 1:
                            g = (j - 1) // 2
                            if g + 2 < NRG:
                                sync.wait_ge(sh.h, sh_b + NCF * (g + 1))
                                xn_load(g + 2)
                    for hh in range(2):
                        for p in range(2):
                            sync.wait_ge(sost[hh][p].h, sost_b[hh][p] + 16 * NR3 * 2)
                    sync.sem_inc(sbar.h, 1)

                @blk.vector
                def _(v):
                    v.wait_ge(sbar.h, bar_in)
                    for g in range(NRG):
                        for c in range(NCF):
                            ci = NCF * g + c
                            slot = ci % 3
                            v.wait_ge(ssc3.h, ssc3_b + ci + 1)
                            if ci >= 2:
                                v.wait_ge(srb.h, srb_b + ci - 1)
                            su2.inc(v.scalar_tensor_tensor(
                                u2_sb[ci % 2][:], ss_sb[ci % 2][:], Rp_sb[:, c:c + 1],
                                psW1[slot][:], ALU.mult, ALU.add))
                            if c == 16:
                                if g == 0:
                                    v.wait_ge(sb2.h, 16)
                                for tt2 in range(2):
                                    j = 2 * g + tt2
                                    v.wait_ge(sxp[j % 3].h,
                                              sxp_b[j % 3] + 16 * (j // 3 + 1))
                                    for hh in range(2):
                                        if j >= 2:
                                            v.wait_ge(sost[hh][j % 2].h,
                                                      sost_b[hh][j % 2] + 16 * (j // 2))
                                        v.tensor_tensor(
                                            out_sb2[j % 2][:, hh * 512:(hh + 1) * 512],
                                            x_sb[j % 3][:, hh * 512:(hh + 1) * 512],
                                            b2q[:, hh * 512:(hh + 1) * 512], ALU.add)
                        for tt in range(2):
                            j = 2 * g + tt
                            v.wait_ge(sw2.h, sw2_b + NCF * (g + 1))
                            v.drain()
                            for hh in range(2):
                                soadd.inc(v.tensor_tensor(
                                    out_sb2[j % 2][:, hh * 512:(hh + 1) * 512],
                                    psOut[tt][:, hh * 512:(hh + 1) * 512],
                                    out_sb2[j % 2][:, hh * 512:(hh + 1) * 512],
                                    ALU.add))
                    v.drain()
                    v.sem_inc(sbar.h, 1)

                @blk.scalar
                def _(sc):
                    sc.wait_ge(sbar.h, bar_in)
                    sc.wait_ge(sprep.h, 1)
                    for g in range(NRG):
                        for c in range(NCF):
                            ci = NCF * g + c
                            slot = ci % 3
                            sc.wait_ge(sh.h, sh_b + ci + 1)
                            ssc3.inc(sc.activation(ss_sb[ci % 2][:], psW1[slot][:],
                                                   AF.Sin, scale=0.1,
                                                   bias=phb_sb[:, c:c + 1]))
                            cp = ci - 1
                            if cp >= 0:
                                sc.wait_ge(su2.h, su2_b + cp + 1)
                                if cp >= 3:
                                    sc.wait_ge(sw2.h, sw2_b + cp - 2)
                                srb.inc(sc.activation(rb_sb[cp % 3][:], u2_sb[cp % 2][:],
                                                      AF.Relu, bias=b1_sb[:, (cp % NCF):(cp % NCF) + 1]))
                    cp = NCHG - 1
                    sc.wait_ge(su2.h, su2_b + cp + 1)
                    sc.wait_ge(sw2.h, sw2_b + cp - 2)
                    srb.inc(sc.activation(rb_sb[cp % 3][:], u2_sb[cp % 2][:],
                                          AF.Relu, bias=b1_sb[:, NCF - 1:NCF]))
                    sc.drain()
                    sc.sem_inc(sbar.h, 1)

                @blk.tensor
                def _(t):
                    t.wait_ge(sbar.h, bar_in)
                    t.wait_ge(swq.h, 48)
                    t.wait_ge(swr.h, 144)
                    t.wait_ge(sc0.h, 128)
                    t.wait_ge(sprep.h, 1)
                    for ci in range(NCHG + 2):
                        if ci < NCHG:
                            g, c = divmod(ci, NCF)
                            if c == 0:
                                t.wait_ge(sxn3p[g % 2].h,
                                          sxn3p_b[g % 2] + 16 * (g // 2 + 1))
                            slot = ci % 3
                            if ci >= 3:
                                t.wait_ge(su2.h, su2_b + ci - 2)
                            for k in range(NCD):
                                ins = t.matmul(
                                    psW1[slot][:],
                                    W1_sb[:, k * DFF + c * 128: k * DFF + (c + 1) * 128],
                                    xn_r3[g % 2][:, k * R3:(k + 1) * R3],
                                    start=(k == 0), stop=(k == NCD - 1),
                                    skip_group_check=True)
                            sh.inc(ins)
                        cw = ci - 2
                        if cw >= 0 and cw % NCF == 0:
                            gg = cw // NCF
                            for tt in range(2):
                                c0 = (gg % NR3) * R3 + tt * 128
                                for h in range(2):
                                    if gg >= 1:
                                        t.wait_ge(soadd.h,
                                                  soadd_b + 2 * (2 * (gg - 1) + tt) + h + 1)
                                    t.matmul(psOut[tt][:, h * 512:(h + 1) * 512],
                                             o_fm2[gg // NR3][:, c0:c0 + 128],
                                             Wo_sb[:, h * 512:(h + 1) * 512],
                                             start=True, stop=False,
                                             skip_group_check=True)
                        if cw >= 0:
                            cc = cw % NCF
                            t.wait_ge(srb.h, srb_b + cw + 1)
                            for tt in range(2):
                                for h in range(2):
                                    ins = t.matmul(
                                        psOut[tt][:, h * 512:(h + 1) * 512],
                                        rb_sb[cw % 3][:, tt * 128:(tt + 1) * 128],
                                        W2_sb[:, cc * D + h * 512: cc * D + (h + 1) * 512],
                                        start=False, stop=(cc == NCF - 1),
                                        skip_group_check=True)
                            sw2.inc(ins)
                    t.drain()
                    t.sem_inc(sbar.h, 1)

          BAR[0] += 4
        except _Done:
            pass

        with nc.Block() as block:
            @block.sync
            def _(sync):
                sync.wait_ge(sc0.h, 128)

            @block.gpsimd
            def _(g):
                g.wait_ge(swq.h, 48)
                g.wait_ge(swr.h, 144)

    return nc


def kernel(L0_attn_x, L0_attn_Wq, L0_attn_Wk, L0_attn_Wv, L0_attn_Wo,
           L0_ffn_W1, L0_ffn_b1, L0_ffn_quantum_phases,
           L0_ffn_quantum_weights_real, L0_ffn_quantum_weights_imag,
           L0_ffn_W2, L0_ffn_b2):
    x = np.ascontiguousarray(np.asarray(L0_attn_x, dtype=np.float32))
    wq = np.ascontiguousarray(np.asarray(L0_attn_Wq, dtype=np.float32))
    wk = np.ascontiguousarray(np.asarray(L0_attn_Wk, dtype=np.float32))
    wv = np.ascontiguousarray(np.asarray(L0_attn_Wv, dtype=np.float32))
    wo = np.ascontiguousarray(np.asarray(L0_attn_Wo, dtype=np.float32))
    w1 = np.ascontiguousarray(np.asarray(L0_ffn_W1, dtype=np.float32))
    w2 = np.ascontiguousarray(np.asarray(L0_ffn_W2, dtype=np.float32))
    b1 = np.asarray(L0_ffn_b1, dtype=np.float32)
    th = np.asarray(L0_ffn_quantum_phases, dtype=np.float32)
    wr = np.asarray(L0_ffn_quantum_weights_real, dtype=np.float32)
    wi = np.asarray(L0_ffn_quantum_weights_imag, dtype=np.float32)
    b2 = np.asarray(L0_ffn_b2, dtype=np.float32)

    per_feat = lambda a: np.ascontiguousarray(a.reshape(NCF, 128).T)
    b1t = per_feat(b1)
    tht = per_feat(th)
    wrt = per_feat(wr)
    wit = per_feat(wi)
    b2rep = np.ascontiguousarray(np.broadcast_to(b2.reshape(1, D),
                                                 (128, D))).astype(np.float32)
    maskd = np.where(np.triu(np.ones((128, 128), dtype=bool), k=1),
                     np.float32(-1e9), np.float32(0.0)).astype(ml_dtypes.bfloat16)
    idn = np.eye(128, dtype=ml_dtypes.bfloat16)
    ones1 = np.ones((1, 128), dtype=ml_dtypes.bfloat16)
    fcst = np.zeros((128, 8), dtype=np.float32)
    fcst[:, 0] = LNEPS
    fcst[:, 1] = -BS

    nc = build_nc()
    in_maps = []
    for i in range(NCORE):
        xc = np.ascontiguousarray(x[NSEQ * i:NSEQ * (i + 1)].reshape(NTOK, D))
        in_maps.append({
            "x": xc, "w1": w1, "w2": w2, "wq": wq, "wk": wk, "wv": wv, "wo": wo,
            "b1t": b1t, "tht": tht, "wrt": wrt, "wit": wit, "b2rep": b2rep,
            "maskd": maskd, "idn": idn, "ones1": ones1, "fcst": fcst,
        })
    res = run_bass_kernel_spmd(nc, in_maps, core_ids=list(range(NCORE)))
    outs = [r["out"].reshape(NSEQ, T, D) for r in res.results]
    return np.ascontiguousarray(np.concatenate(outs, axis=0))


# version 38
# speedup vs baseline: 1.0160x; 1.0010x over previous
import os
import numpy as np
import ml_dtypes
from contextlib import ExitStack

import concourse.bass as bass
import concourse.mybir as mybir
from concourse.bass_utils import run_bass_kernel_spmd

F32 = mybir.dt.float32
BF16 = mybir.dt.bfloat16
AF = mybir.ActivationFunctionType
ALU = mybir.AluOpType
AX = mybir.AxisListType

NCORE = 8
T = 2048
D = 1024
DK = 128
DFF = 4096
NSEQ = 2
NTOK = NSEQ * T
NCD = D // 128
NCF = DFF // 128
NT1 = T // 128
R1 = 512
NR1 = T // R1
R3 = 256
NR3 = T // R3
QSCALE = float(DK) ** -0.5
BS = 12.0
LNEPS = 1e-5


class Ctr:

    def __init__(self, h):
        self.h = h
        self.n = 0

    def inc(self, ins, k=1):
        ins.then_inc(self.h, k)
        self.n += k
        return self.n


def build_nc():
    nc = bass.Bass("TRN2", target_bir_lowering=False)

    x_in = nc.dram_tensor("x", [NTOK, D], F32, kind="ExternalInput").ap()
    w1 = nc.dram_tensor("w1", [D, DFF], F32, kind="ExternalInput").ap()
    w2 = nc.dram_tensor("w2", [DFF, D], F32, kind="ExternalInput").ap()
    wq = nc.dram_tensor("wq", [D, DK], F32, kind="ExternalInput").ap()
    wk = nc.dram_tensor("wk", [D, DK], F32, kind="ExternalInput").ap()
    wv = nc.dram_tensor("wv", [D, DK], F32, kind="ExternalInput").ap()
    wo = nc.dram_tensor("wo", [DK, D], F32, kind="ExternalInput").ap()
    b1t = nc.dram_tensor("b1t", [128, NCF], F32, kind="ExternalInput").ap()
    tht = nc.dram_tensor("tht", [128, NCF], F32, kind="ExternalInput").ap()
    wrt = nc.dram_tensor("wrt", [128, NCF], F32, kind="ExternalInput").ap()
    wit = nc.dram_tensor("wit", [128, NCF], F32, kind="ExternalInput").ap()
    b2rep = nc.dram_tensor("b2rep", [128, D], F32, kind="ExternalInput").ap()
    maskd = nc.dram_tensor("maskd", [128, 128], BF16, kind="ExternalInput").ap()
    idn = nc.dram_tensor("idn", [128, 128], BF16, kind="ExternalInput").ap()
    ones1 = nc.dram_tensor("ones1", [1, 128], BF16, kind="ExternalInput").ap()
    fcst = nc.dram_tensor("fcst", [128, 8], F32, kind="ExternalInput").ap()
    out = nc.dram_tensor("out", [NTOK, D], F32, kind="ExternalOutput").ap()
    _xnkind = "ExternalOutput" if os.environ.get("KDBG_XN") else "Internal"
    xn_scr = [nc.dram_tensor(f"xn_scr{i}", [128, NCD, T], BF16, kind=_xnkind).ap()
              for i in range(NSEQ)]

    with ExitStack() as ctx:
        sbuf = lambda name, shape, dt: ctx.enter_context(nc.sbuf_tensor(name, shape, dt))
        sem = lambda name: Ctr(ctx.enter_context(nc.semaphore(name)))

        W1_sb = sbuf("W1_sb", [128, NCD * DFF], BF16)
        W2_sb = sbuf("W2_sb", [128, NCF * D], BF16)
        Wq_sb = sbuf("Wq_sb", [128, NCD * DK], BF16)
        Wk_sb = sbuf("Wk_sb", [128, NCD * DK], BF16)
        Wv_sb = sbuf("Wv_sb", [128, NCD * DK], BF16)
        Wo_sb = sbuf("Wo_sb", [128, D], BF16)
        b1_sb = sbuf("b1_sb", [128, NCF], F32)
        th_sb = sbuf("th_sb", [128, NCF], F32)
        wr_sb = sbuf("wr_sb", [128, NCF], F32)
        wi_sb = sbuf("wi_sb", [128, NCF], F32)
        Rp_sb = sbuf("Rp_sb", [128, NCF], F32)
        phb_sb = sbuf("phb_sb", [128, NCF], F32)
        prep = sbuf("prep", [128, 8 * NCF], F32)
        mask_sb = sbuf("mask_sb", [128, 128], BF16)
        idn_sb = sbuf("idn_sb", [128, 128], BF16)
        ones_sb = sbuf("ones_sb", [1, 128], BF16)
        diag_li = [sbuf(f"diag_li{i}", [128, 128], BF16) for i in range(2)]
        fc_sb = sbuf("fc_sb", [128, 8], F32)
        stats = sbuf("stats", [128, 64], F32)

        x_sb = [sbuf(f"x_sb{i}", [128, D], F32) for i in range(3)]

        xn_rg = sbuf("xn_rg", [128, NCD * R1], BF16)
        Q_fm = sbuf("Q_fm", [128, T], BF16)
        b2q = Q_fm[:].bitcast(F32)
        K_fm = sbuf("K_fm", [128, T], BF16)
        V_tm = sbuf("V_tm", [128, NT1 * DK], BF16)
        o_fm2 = [sbuf(f"o_fm{i}", [128, T], BF16) for i in range(2)]
        probs = sbuf("probs", [128, T], BF16)
        xn_tm = [probs[:, 0:D], probs[:, D:2 * D], None]
        pT_big = [sbuf(f"pT_big{i}", [128, 1024], BF16) for i in range(2)]
        xn_tm[2] = pT_big[1][:]
        xn_r3 = [sbuf(f"xn_r3{i}", [128, NCD * R3], BF16) for i in range(2)]
        x_sb.append(xn_r3[0][:].bitcast(F32))
        x_sb.append(xn_r3[1][:].bitcast(F32))
        rb_sb = [sbuf(f"rb_sb{i}", [128, R3], BF16) for i in range(3)]
        ss_sb = [sbuf(f"ss_sb{i}", [128, R3], BF16) for i in range(2)]
        u2_sb = [sbuf(f"u2_sb{i}", [128, R3], F32) for i in range(2)]
        out_sb = sbuf("out_sb", [128, D], F32)
        out_sb2 = [out_sb, probs[:].bitcast(F32)]

        sc0 = sem("sc0")
        swq = sem("swq")
        swr = sem("swr")
        sbar = sem("sbar")
        sxp = [sem(f"sxp{i}") for i in range(5)]
        ssq = sem("ssq")
        svar = sem("svar")
        srst = sem("srst")
        sln = sem("sln")
        sdg3 = sem("sdg3")
        stx = sem("stx")
        stxe = sem("stxe")
        sxnst = [sem(f"sxnst{i}") for i in range(4)]
        sqk = sem("sqk")
        sqke = sem("sqke")
        slg = sem("slg")
        smd = sem("smd")
        se1 = sem("se1")
        se1B = sem("se1B")
        se2 = sem("se2")
        stx2 = sem("stx2")
        slgA = sem("slgA")
        st2e = sem("st2e")
        st2eV = sem("st2eV")
        spv = sem("spv")
        soe = sem("soe")
        sxn3p = [sem("sxn3p0"), sem("sxn3p1")]
        sh = sem("sh")
        ssc3 = sem("ssc3")
        su2 = sem("su2")
        srb = sem("srb")
        sw2 = sem("sw2")
        soadd = sem("soadd")
        sost = [[sem(f"sost{h}{p}") for p in range(2)] for h in range(2)]
        sp1 = sem("sp1")
        sp2 = sem("sp2")
        sp3 = sem("sp3")
        sb2 = sem("sb2")
        sprep = sem("sprep")

        BAR = [0]
        _plim = int(os.environ.get("KDBG_PHASES", "99"))

        class _Done(Exception):
            pass


        try:
          for s in range(NSEQ):
            xb = s * T
            if 3 * s + 1 > _plim:
                raise _Done()
            with ExitStack() as pctx:
                psa = lambda name, shape, dt: pctx.enter_context(
                    nc.psum_tensor(name, shape, dt))
                psTxB = [psa(f"psTx{s}_{i}", [128, 1024], F32) for i in range(2)]
                psQ = psa(f"psQ{s}", [128, R1], F32)
                psK = psa(f"psK{s}", [128, R1], F32)
                psVB = psa(f"psVB{s}", [128, R1], F32)
                blk = pctx.enter_context(nc.Block(no_gpsimd_drain=True))

                sxp_b = [sxp[i].n for i in range(5)]
                ssq_b = ssq.n
                svar_b = svar.n
                srst_b = srst.n
                sln_b = sln.n
                sdg3_b = sdg3.n
                stx_b = stx.n
                stxe_b = stxe.n
                sxnst_b = [sxnst[i].n for i in range(4)]
                sqk_b = sqk.n
                sqke_b = sqke.n
                if s == 0:
                    sxn3p_pre_seq = {}
                bar_in = BAR[0]

                @blk.sync
                def _(sync):
                    if bar_in:
                        sync.wait_ge(sbar.h, bar_in)
                    if s == 0:
                        for dst, srcd in [(fc_sb, fcst), (idn_sb, idn),
                                          (mask_sb, maskd), (ones_sb, ones1),
                                          (b1_sb, b1t), (th_sb, tht),
                                          (wr_sb, wrt), (wi_sb, wit)]:
                            sc0.inc(sync.dma_start(dst[:], srcd), 16)
                    for j in range(NT1):
                        if j >= 5:
                            sync.wait_ge(sln.h, sln_b + j - 4)
                            sync.wait_ge(svar.h, svar_b + j - 4)
                        sxp[j % 5].inc(sync.dma_start(
                            x_sb[j % 5][:], x_in[xb + j * 128: xb + (j + 1) * 128, :]), 16)
                    for pp in range(5):
                        sync.wait_ge(sxp[pp].h, sxp_b[pp] + 16 * len(
                            [j for j in range(NT1) if j % 5 == pp]))
                    sync.sem_inc(sbar.h, 1)

                @blk.gpsimd
                def _(g):
                    if bar_in:
                        g.wait_ge(sbar.h, bar_in)
                    if s == 0:
                        for w_hbm, w_sb in [(wq, Wq_sb), (wk, Wk_sb), (wv, Wv_sb)]:
                            srcd = w_hbm.rearrange("(c p) k -> p c k", c=NCD, p=128)
                            dstd = w_sb[:].rearrange("p (c k) -> p c k", c=NCD)
                            swq.inc(g.dma_start(dstd, srcd), 16)
                        swr.inc(g.dma_start(Wo_sb[:], wo), 16)
                    for j in range(NT1):
                        g.wait_ge(stxe.h, stxe_b + j + 1)
                        src = xn_rg[:, (j % 4) * 1024:(j % 4 + 1) * 1024].rearrange(
                            "p (c n) -> p c n", c=NCD)
                        dst = xn_scr[s][:, :, j * 128:(j + 1) * 128]
                        sxnst[j % 4].inc(g.dma_start(dst, src), 16)
                    if s == 0:
                        for jw in range(4):
                            srcw = w1.rearrange("(a c p) f -> a p c f",
                                                a=4, c=2, p=128)[jw]
                            dstw = W1_sb[:, jw * 2 * DFF:(jw + 1) * 2 * DFF].rearrange(
                                "p (c f) -> p c f", c=2)
                            swr.inc(g.dma_start(dstw, srcw), 16)
                    else:
                        for jw in range(4):
                            srcw = w2.rearrange("(a c p) d -> a p c d",
                                                a=4, c=8, p=128)[jw]
                            dstw = W2_sb[:, jw * 8 * D:(jw + 1) * 8 * D].rearrange(
                                "p (c d) -> p c d", c=8)
                            swr.inc(g.dma_start(dstw, srcw), 16)
                    for i in range(4):
                        g.wait_ge(sxnst[i].h, sxnst_b[i] + 16 * NR1)
                    g.sem_inc(sbar.h, 1)

                NF = NCF
                col = lambda i: prep[:, i * NF:(i + 1) * NF]
                pv = {}
                if s == 0:
                    vops = [lambda v: v.wait_ge(sp1.h, 2)]

                    def _o(fn):
                        vops.append(lambda v, fn=fn: (v.drain(), fn(v)))
                    _o(lambda v: v.tensor_scalar(col(1), col(6), -2.0, 1.0,
                                                 ALU.mult, ALU.add))
                    _o(lambda v: v.tensor_copy(col(0), col(5)))
                    for _ in range(4):
                        _o(lambda v: v.scalar_tensor_tensor(col(2), col(0), 2.0,
                                                            col(1), ALU.mult, ALU.mult))
                        _o(lambda v: v.tensor_tensor(col(3), col(0), col(0), ALU.mult))
                        _o(lambda v: v.tensor_scalar(col(1), col(3), -2.0, 1.0,
                                                     ALU.mult, ALU.add))
                        _o(lambda v: v.tensor_copy(col(0), col(2)))
                    _o(lambda v: v.tensor_tensor(col(2), col(1), wr_sb[:], ALU.mult))
                    _o(lambda v: v.tensor_tensor(col(3), col(0), wi_sb[:], ALU.mult))
                    _o(lambda v: v.tensor_tensor(col(2), col(2), col(3), ALU.add))
                    _o(lambda v: v.tensor_tensor(col(3), col(2), wi_sb[:], ALU.mult))
                    _o(lambda v: v.tensor_scalar(col(4), col(1), -1.0, None, ALU.mult))
                    _o(lambda v: v.tensor_tensor(col(4), col(1), col(4), ALU.max))
                    _o(lambda v: v.tensor_scalar(col(4), col(4), 1.0, None, ALU.add))
                    _o(lambda v: v.reciprocal(col(5), col(4)))
                    _o(lambda v: sp2.inc(v.tensor_tensor(col(4), col(0), col(5),
                                                         ALU.mult)))
                    for i, fn in enumerate(vops):
                        pv.setdefault(min(i // 3, 9), []).append(fn)

                    def _mid(v):
                        v.wait_ge(sp3.h, 1)
                        v.drain()
                        sp2.inc(v.tensor_tensor(col(5), col(4), col(7), ALU.mult))
                    pv.setdefault(13, []).append(_mid)

                    def _tail(v):
                        v.wait_ge(sp3.h, 2)
                        v.drain()
                        v.scalar_tensor_tensor(Rp_sb[:], col(3), 0.1, col(7),
                                               ALU.mult, ALU.mult)
                        v.drain()
                        v.tensor_scalar(col(5), col(6), 2.0, None, ALU.mult)
                        v.drain()
                        ins = v.scalar_tensor_tensor(phb_sb[:], b1_sb[:], 0.1,
                                                     col(5), ALU.mult, ALU.add)
                        sprep.inc(ins)
                    pv.setdefault(15, []).append(_tail)

                @blk.vector
                def _(v):
                    if bar_in:
                        v.wait_ge(sbar.h, bar_in)
                    for j in range(NT1 + 2):
                        if j < NT1:
                            p2 = 16 * (j % 4)
                            v.wait_ge(sxp[j % 5].h, sxp_b[j % 5] + 16 * (j // 5 + 1))
                            v.bn_stats(stats[:, p2 + 0:p2 + 6], x_sb[j % 5][:, 0:512])
                            v.bn_stats(stats[:, p2 + 6:p2 + 12], x_sb[j % 5][:, 512:1024])
                            v.drain()
                            ins = v.bn_aggr(stats[:, p2 + 12:p2 + 14],
                                            stats[:, p2 + 0:p2 + 12])
                            svar.inc(ins)
                        jj = j - 2
                        if jj >= 0:
                            p2 = 16 * (jj % 4)
                            v.wait_ge(srst.h, srst_b + jj + 1)
                            v.reciprocal(stats[:, p2 + 14:p2 + 15],
                                         stats[:, p2 + 15:p2 + 16])
                            if jj >= 2:
                                v.wait_ge(stx.h, stx_b + jj - 1)
                            v.drain()
                            ins = v.tensor_scalar(
                                diag_li[jj % 2][:], idn_sb[:],
                                stats[:, p2 + 14:p2 + 15], -1.0,
                                ALU.mult, ALU.mult)
                            sdg3.inc(ins)
                        if j < NT1:
                            for fn in pv.get(j, []):
                                fn(v)
                    v.drain()
                    v.sem_inc(sbar.h, 1)

                @blk.scalar
                def _(sc):
                    if bar_in:
                        sc.wait_ge(sbar.h, bar_in)
                    sc.wait_ge(sc0.h, 128)
                    if s == 0:
                        sc.activation(col(5), th_sb[:], AF.Sin, scale=1.0 / 16)
                        sc.activation(col(4), th_sb[:], AF.Sin, scale=1.0 / 32)
                        sc.drain()
                        sp1.inc(sc.activation(col(6), col(4), AF.Square), 2)
                    for j in range(NT1 + 2):
                        jj = j - 1
                        if s == 0 and j == 12:
                            sc.wait_ge(sp2.h, 1)
                            sc.drain()
                            sp3.inc(sc.sign(col(7), col(1)))
                        if s == 0 and j == 14:
                            sc.wait_ge(sp2.h, 2)
                            sc.drain()
                            sp3.inc(sc.activation(col(6), col(5), AF.Arctan))
                        if 0 <= jj < NT1:
                            p2 = 16 * (jj % 4)
                            sc.wait_ge(svar.h, svar_b + jj + 1)
                            srst.inc(sc.activation(stats[:, p2 + 15:p2 + 16],
                                                   stats[:, p2 + 13:p2 + 14],
                                                   AF.Sqrt, bias=fc_sb[:, 0:1]))
                            if jj >= 3:
                                sc.wait_ge(stx.h, stx_b + jj - 2)
                            ins = sc.activation(xn_tm[jj % 3], x_sb[jj % 5][:],
                                                AF.Identity, scale=-1.0,
                                                bias=stats[:, p2 + 12:p2 + 13])
                            sln.inc(ins)
                        je = j - 2
                        if 0 <= je < NT1:
                            r, tt = je // 4, je % 4
                            if tt == 0 and r > 0:
                                sc.wait_ge(sqk.h, sqk_b + 4 * r)
                            if je >= 4:
                                sc.wait_ge(sxnst[je % 4].h,
                                           sxnst_b[je % 4] + 16 * (je // 4))
                            sc.wait_ge(stx.h, stx_b + je + 1)
                            stxe.inc(sc.copy(xn_rg[:, tt * 1024:(tt + 1) * 1024],
                                             psTxB[je % 2][:]))
                            if tt == 0 and r >= 1:
                                rp = r - 1
                                sc.wait_ge(sqk.h, sqk_b + 4 * (rp + 1))
                                sqke.inc(sc.activation(Q_fm[:, rp * R1:(rp + 1) * R1],
                                                       psQ[:], AF.Copy, scale=QSCALE))
                                sqke.inc(sc.copy(K_fm[:, rp * R1:(rp + 1) * R1], psK[:]))
                                sqke.inc(sc.copy(V_tm[:, rp * R1:(rp + 1) * R1], psVB[:]))

                    rp = NR1 - 1
                    sc.wait_ge(sqk.h, sqk_b + 4 * (rp + 1))
                    sqke.inc(sc.activation(Q_fm[:, rp * R1:(rp + 1) * R1],
                                           psQ[:], AF.Copy, scale=QSCALE))
                    sqke.inc(sc.copy(K_fm[:, rp * R1:(rp + 1) * R1], psK[:]))
                    sqke.inc(sc.copy(V_tm[:, rp * R1:(rp + 1) * R1], psVB[:]))
                    sc.drain()
                    sc.sem_inc(sbar.h, 1)

                @blk.tensor
                def _(t):
                    if bar_in:
                        t.wait_ge(sbar.h, bar_in)
                    t.wait_ge(sc0.h, 128)
                    if s == 0:
                        t.wait_ge(swq.h, 48)
                    def qkv_slice(jq):
                        tt, r = jq % 4, jq // 4
                        if tt == 0 and r > 0:
                            t.wait_ge(sqke.h, sqke_b + 3 * r)
                        t.wait_ge(stxe.h, stxe_b + jq + 1)
                        for ci in range(NCD):
                            t.matmul(psQ[:, tt * 128:(tt + 1) * 128],
                                     Wq_sb[:, ci * DK:(ci + 1) * DK],
                                     xn_rg[:, tt * 1024 + ci * 128:
                                            tt * 1024 + (ci + 1) * 128],
                                     start=(ci == 0), stop=(ci == NCD - 1),
                                     skip_group_check=True)
                        for ci in range(NCD):
                            t.matmul(psK[:, tt * 128:(tt + 1) * 128],
                                     Wk_sb[:, ci * DK:(ci + 1) * DK],
                                     xn_rg[:, tt * 1024 + ci * 128:
                                            tt * 1024 + (ci + 1) * 128],
                                     start=(ci == 0), stop=(ci == NCD - 1),
                                     skip_group_check=True)
                        for ci in range(NCD):
                            ins = t.matmul(psVB[:, tt * 128:(tt + 1) * 128],
                                           xn_rg[:, tt * 1024 + ci * 128:
                                                  tt * 1024 + (ci + 1) * 128],
                                           Wv_sb[:, ci * DK:(ci + 1) * DK],
                                           start=(ci == 0), stop=(ci == NCD - 1),
                                           skip_group_check=True)
                        sqk.inc(ins)

                    for j in range(NT1):
                        t.wait_ge(sln.h, sln_b + j + 1)
                        t.wait_ge(sdg3.h, sdg3_b + j + 1)
                        if j >= 2:
                            t.wait_ge(stxe.h, stxe_b + j - 1)
                        for c in range(NCD):
                            ins = t.matmul(psTxB[j % 2][:, c * 128:(c + 1) * 128],
                                           xn_tm[j % 3][:, c * 128:(c + 1) * 128],
                                           diag_li[j % 2][:],
                                           start=True, stop=True,
                                           skip_group_check=True)
                        stx.inc(ins)
                        if j >= 1:
                            qkv_slice(j - 1)
                    qkv_slice(NT1 - 1)
                    t.drain()
                    t.sem_inc(sbar.h, 1)

            BAR[0] += 5
            if 3 * s + 2 > _plim:
                raise _Done()

            with ExitStack() as pctx:
                psa = lambda name, shape, dt: pctx.enter_context(
                    nc.psum_tensor(name, shape, dt))
                psLogA = psa(f"psLogA{s}", [128, 1024], F32)
                psLogB = psa(f"psLogB{s}", [128, 1024], F32)
                psT2 = [psa(f"psT2{s}_{i}", [128, 512], F32) for i in range(2)]
                psO = [psa(f"psO{s}_{i}", [128, 128], F32) for i in range(2)]
                blk = pctx.enter_context(nc.Block())

                slg_b = slg.n
                se1_b = se1.n
                se1B_b = se1B.n
                se2_b = se2.n
                stx2_b = stx2.n
                st2e_b = st2e.n
                st2eV_b = st2eV.n
                spv_b = spv.n
                soe_b = soe.n
                bar_in = BAR[0]
                cum = [0]
                for qi in range(NT1):
                    cum.append(cum[-1] + qi + 1)

                slgA_b = slgA.n
                p2buf = [xn_rg[:, 0:T], xn_rg[:, T:2 * T]]
                nsg = [j // 4 + 1 for j in range(NT1)]
                cum_sg = [0]
                for j in range(NT1):
                    cum_sg.append(cum_sg[-1] + nsg[j])
                segs = [(j, si) for j in range(NT1) for si in range(nsg[j])]
                bank_of, ord_in_bank = {}, {}
                _cnt = [0, 0]
                for g, key in enumerate(segs):
                    b = g % 2
                    bank_of[key] = b
                    _cnt[b] += 1
                    ord_in_bank[key] = _cnt[b]
                segw2 = lambda j, si: (min(j + 1 - si * 4, 4)) * 128
                ev_sem = lambda b: (st2e, st2e_b) if b == 0 else (st2eV, st2eV_b)

                @blk.tensor
                def _(t):
                    t.wait_ge(sbar.h, bar_in)
                    for qq in range(NT1 + 1):
                        j = qq - 1
                        if qq < NT1:
                            qi = qq
                            if qi >= 1:
                                t.wait_ge(se1.h, se1_b + qi)
                            qsl = Q_fm[:, qi * 128:(qi + 1) * 128]
                            if qi >= 8:
                                for kb in range(2):
                                    ins = t.matmul(psLogA[:, kb * R1:(kb + 1) * R1],
                                                   qsl, K_fm[:, kb * R1:(kb + 1) * R1],
                                                   start=True, stop=True)
                                slgA.inc(ins)
                                if qi >= 9:
                                    t.wait_ge(se1B.h, se1B_b + qi - 8)
                            qb = qi - 8 if qi >= 8 else qi
                            ps_h = psLogB if qi >= 8 else psLogA
                            cb = 1024 if qi >= 8 else 0
                            nf = qb // 4
                            for kb in range(nf):
                                t.matmul(ps_h[:, kb * R1:(kb + 1) * R1], qsl,
                                         K_fm[:, cb + kb * R1: cb + (kb + 1) * R1],
                                         start=True, stop=True)
                            for m in range(qb % 4 + 1):
                                c0 = nf * R1 + m * 128
                                t.matmul(ps_h[:, c0:c0 + 128], qsl,
                                         K_fm[:, cb + c0: cb + c0 + 128],
                                         start=(m == 0), stop=False,
                                         skip_group_check=True)
                            ins = t.matmul(ps_h[:, qb * 128:(qb + 1) * 128],
                                           idn_sb[:], mask_sb[:],
                                           start=False, stop=True,
                                           skip_group_check=True)
                            slg.inc(ins)
                        if j >= 0:
                            t.wait_ge(se1.h, se1_b + j + 1)
                            if j >= 8:
                                t.wait_ge(se1B.h, se1B_b + j - 7)
                            t.wait_ge(se2.h, se2_b + j + 1)
                            for si in range(nsg[j]):
                                g = cum_sg[j] + si
                                b = g % 2
                                if g >= 2:
                                    pk = segs[g - 2]
                                    sem, base = ev_sem(bank_of[pk])
                                    t.wait_ge(sem.h, base + ord_in_bank[pk])
                                for c in range(si * 4, min(j + 1, si * 4 + 4)):
                                    ins = t.matmul(
                                        psT2[b][:, (c % 4) * 128:(c % 4 + 1) * 128],
                                        p2buf[j % 2][:, c * 128:(c + 1) * 128],
                                        diag_li[j % 2][:],
                                        start=True, stop=True,
                                        skip_group_check=True)
                                stx2.inc(ins)
                        if j >= 0:
                            if j >= 2:
                                t.wait_ge(soe.h, soe_b + j - 1)
                            for si in range(nsg[j]):
                                sem, base = ev_sem(bank_of[(j, si)])
                                t.wait_ge(sem.h, base + ord_in_bank[(j, si)])
                                for c in range(si * 4, min(j + 1, si * 4 + 4)):
                                    ins = t.matmul(psO[j % 2][:],
                                                   V_tm[:, c * DK:(c + 1) * DK],
                                                   pT_big[c // 8][:, (c % 8) * 128:(c % 8 + 1) * 128],
                                                   start=(c == 0), stop=(c == j),
                                                   skip_group_check=True)
                            spv.inc(ins)
                    t.drain()
                    t.sem_inc(sbar.h, 1)

                if s == 1:
                    sxn3p_pre_seq[0] = [sxn3p[0].n, sxn3p[1].n]

                    @blk.sync
                    def _(sync):
                        sync.wait_ge(sbar.h, bar_in)
                        for gg in range(2):
                            srcp = xn_scr[0][:, :, gg * R3:(gg + 1) * R3]
                            dstp = xn_r3[gg][:].rearrange("p (c n) -> p c n", c=NCD)
                            sxn3p[gg].inc(sync.dma_start(dstp, srcp), 16)
                        sync.sem_inc(sbar.h, 1)

                @blk.vector
                def _(v):
                    v.wait_ge(sbar.h, bar_in)
                    v.wait_ge(sc0.h, 128)
                    for j in range(NT1):
                        v.wait_ge(se1.h, se1_b + j + 1)
                        if j >= 8:
                            v.wait_ge(se1B.h, se1B_b + j - 7)
                        v.drain()
                        a0 = 16 + 4 * (j % 2)
                        if j >= 8:
                            v.tensor_tensor(stats[:, 22:23], stats[:, a0:a0 + 1],
                                            stats[:, a0 + 1:a0 + 2], ALU.add)
                        else:
                            v.tensor_copy(stats[:, 22:23], stats[:, a0:a0 + 1])
                        v.drain()
                        v.reciprocal(stats[:, 23:24], stats[:, 22:23])
                        if j >= 2:
                            v.wait_ge(stx2.h, stx2_b + cum_sg[j - 1])
                        v.drain()
                        ins = v.tensor_scalar(diag_li[j % 2][:], idn_sb[:],
                                              stats[:, 23:24], None, ALU.mult)
                        se2.inc(ins)
                        for si in range(nsg[j]):
                            b = bank_of[(j, si)]
                            v.wait_ge(stx2.h, stx2_b + cum_sg[j] + si + 1)
                            if j >= 1:
                                v.wait_ge(spv.h, spv_b + j)
                            v.drain()
                            w = segw2(j, si)
                            (st2e if b == 0 else st2eV).inc(v.tensor_copy(
                                pT_big[si // 2][:, (si % 2) * 512:(si % 2) * 512 + w],
                                psT2[b][:, 0:w]))
                    v.drain()
                    v.sem_inc(sbar.h, 1)

                @blk.scalar
                def _(sc):
                    sc.wait_ge(sbar.h, bar_in)
                    for qq in range(NT1 + 1):
                        j = qq - 1
                        if qq < NT1:
                            qi = qq
                            if qi >= 2:
                                sc.wait_ge(stx2.h, stx2_b + cum_sg[qi - 1])
                            sc.drain()
                            if qi >= 8:
                                sc.wait_ge(slgA.h, slgA_b + qi - 7)
                                a0 = 16 + 4 * (qi % 2)
                                ins = sc.activation(p2buf[qi % 2][:, 0:1024],
                                                    psLogA[:], AF.Exp,
                                                    bias=fc_sb[:, 1:2],
                                                    accum_out=stats[:, a0:a0 + 1])
                                se1.inc(ins)
                                sc.wait_ge(slg.h, slg_b + qi + 1)
                                sc.drain()
                                ins = sc.activation(
                                    p2buf[qi % 2][:, 1024:(qi + 1) * 128],
                                    psLogB[:, 0:(qi - 7) * 128], AF.Exp,
                                    bias=fc_sb[:, 1:2],
                                    accum_out=stats[:, a0 + 1:a0 + 2])
                                se1B.inc(ins)
                            else:
                                sc.wait_ge(slg.h, slg_b + qi + 1)
                                a0 = 16 + 4 * (qi % 2)
                                ins = sc.activation(p2buf[qi % 2][:, 0:(qi + 1) * 128],
                                                    psLogA[:, 0:(qi + 1) * 128],
                                                    AF.Exp, bias=fc_sb[:, 1:2],
                                                    accum_out=stats[:, a0:a0 + 1])
                                se1.inc(ins)
                        j2 = qq - 1
                        if j2 >= 0:
                            sc.wait_ge(spv.h, spv_b + j2 + 1)
                            soe.inc(sc.copy(o_fm2[s][:, j2 * 128:(j2 + 1) * 128],
                                            psO[j2 % 2][:]))
                    sc.drain()
                    sc.sem_inc(sbar.h, 1)

            BAR[0] += 3 + (1 if s == 1 else 0)

          if 3 > _plim:
            raise _Done()

          with ExitStack() as pctx:
                psa = lambda name, shape, dt: pctx.enter_context(
                    nc.psum_tensor(name, shape, dt))
                psW1 = [psa(f"psW1_{i}", [128, R3], F32) for i in range(3)]
                psOut = [psa(f"psOut_{i}", [128, D], F32) for i in range(2)]
                blk = pctx.enter_context(nc.Block())

                NRG = 2 * NR3
                NTJ = 2 * 2 * NR3
                NCHG = NRG * NCF
                sxn3p_b = sxn3p_pre_seq[0]
                sxp_b = [sxp[i].n for i in range(3)]
                sh_b = sh.n
                ssc3_b = ssc3.n
                su2_b = su2.n
                srb_b = srb.n
                sw2_b = sw2.n
                soadd_b = soadd.n
                sost_b = [[sost[h][p].n for p in range(2)] for h in range(2)]
                sxnst_b3 = [sxnst[i].n for i in range(4)]
                bar_in = BAR[0]

                @blk.sync
                def _(sync):
                    sync.wait_ge(sbar.h, bar_in)
                    for i in range(4):
                        sync.wait_ge(sxnst[i].h, sxnst_b3[i])
                    sb2.inc(sync.dma_start(b2q, b2rep), 16)

                    def xn_load(g):
                        srcp = xn_scr[g // NR3][:, :, (g % NR3) * R3:
                                               (g % NR3 + 1) * R3]
                        dstp = xn_r3[g % 2][:].rearrange("p (c n) -> p c n", c=NCD)
                        sxn3p[g % 2].inc(sync.dma_start(dstp, srcp), 16)

                    def x_load(j):
                        sxp[j % 3].inc(sync.dma_start(
                            x_sb[j % 3][:], x_in[j * 128:(j + 1) * 128, :]), 16)

                    x_load(0)
                    x_load(1)
                    x_load(2)
                    for j in range(NTJ):
                        r0 = j * 128
                        for hh in range(2):
                            sync.wait_ge(soadd.h, soadd_b + 2 * j + hh + 1)
                            sost[hh][j % 2].inc(sync.dma_start(
                                out[r0:r0 + 128, hh * 512:(hh + 1) * 512],
                                out_sb2[j % 2][:, hh * 512:(hh + 1) * 512]), 16)
                        if j + 3 < NTJ:
                            x_load(j + 3)
                        if j % 2 == 1:
                            g = (j - 1) // 2
                            if g + 2 < NRG:
                                sync.wait_ge(sh.h, sh_b + NCF * (g + 1))
                                xn_load(g + 2)
                    for hh in range(2):
                        for p in range(2):
                            sync.wait_ge(sost[hh][p].h, sost_b[hh][p] + 16 * NR3 * 2)
                    sync.sem_inc(sbar.h, 1)

                @blk.vector
                def _(v):
                    v.wait_ge(sbar.h, bar_in)
                    for g in range(NRG):
                        for c in range(NCF):
                            ci = NCF * g + c
                            slot = ci % 3
                            v.wait_ge(ssc3.h, ssc3_b + ci + 1)
                            if ci >= 2:
                                v.wait_ge(srb.h, srb_b + ci - 1)
                            su2.inc(v.scalar_tensor_tensor(
                                u2_sb[ci % 2][:], ss_sb[ci % 2][:], Rp_sb[:, c:c + 1],
                                psW1[slot][:], ALU.mult, ALU.add))
                            if c == 16:
                                if g == 0:
                                    v.wait_ge(sb2.h, 16)
                                for tt2 in range(2):
                                    j = 2 * g + tt2
                                    v.wait_ge(sxp[j % 3].h,
                                              sxp_b[j % 3] + 16 * (j // 3 + 1))
                                    for hh in range(2):
                                        if j >= 2:
                                            v.wait_ge(sost[hh][j % 2].h,
                                                      sost_b[hh][j % 2] + 16 * (j // 2))
                                        v.tensor_tensor(
                                            out_sb2[j % 2][:, hh * 512:(hh + 1) * 512],
                                            x_sb[j % 3][:, hh * 512:(hh + 1) * 512],
                                            b2q[:, hh * 512:(hh + 1) * 512], ALU.add)
                        for tt in range(2):
                            j = 2 * g + tt
                            v.wait_ge(sw2.h, sw2_b + NCF * (g + 1))
                            v.drain()
                            for hh in range(2):
                                soadd.inc(v.tensor_tensor(
                                    out_sb2[j % 2][:, hh * 512:(hh + 1) * 512],
                                    psOut[tt][:, hh * 512:(hh + 1) * 512],
                                    out_sb2[j % 2][:, hh * 512:(hh + 1) * 512],
                                    ALU.add))
                    v.drain()
                    v.sem_inc(sbar.h, 1)

                @blk.scalar
                def _(sc):
                    sc.wait_ge(sbar.h, bar_in)
                    sc.wait_ge(sprep.h, 1)
                    for g in range(NRG):
                        for c in range(NCF):
                            ci = NCF * g + c
                            slot = ci % 3
                            sc.wait_ge(sh.h, sh_b + ci + 1)
                            ssc3.inc(sc.activation(ss_sb[ci % 2][:], psW1[slot][:],
                                                   AF.Sin, scale=0.1,
                                                   bias=phb_sb[:, c:c + 1]))
                            cp = ci - 1
                            if cp >= 0:
                                sc.wait_ge(su2.h, su2_b + cp + 1)
                                if cp >= 3:
                                    sc.wait_ge(sw2.h, sw2_b + cp - 2)
                                srb.inc(sc.activation(rb_sb[cp % 3][:], u2_sb[cp % 2][:],
                                                      AF.Relu, bias=b1_sb[:, (cp % NCF):(cp % NCF) + 1]))
                    cp = NCHG - 1
                    sc.wait_ge(su2.h, su2_b + cp + 1)
                    sc.wait_ge(sw2.h, sw2_b + cp - 2)
                    srb.inc(sc.activation(rb_sb[cp % 3][:], u2_sb[cp % 2][:],
                                          AF.Relu, bias=b1_sb[:, NCF - 1:NCF]))
                    sc.drain()
                    sc.sem_inc(sbar.h, 1)

                @blk.tensor
                def _(t):
                    t.wait_ge(sbar.h, bar_in)
                    t.wait_ge(swq.h, 48)
                    t.wait_ge(swr.h, 144)
                    t.wait_ge(sc0.h, 128)
                    t.wait_ge(sprep.h, 1)
                    for ci in range(NCHG + 2):
                        if ci < NCHG:
                            g, c = divmod(ci, NCF)
                            if c == 0:
                                t.wait_ge(sxn3p[g % 2].h,
                                          sxn3p_b[g % 2] + 16 * (g // 2 + 1))
                            slot = ci % 3
                            if ci >= 3:
                                t.wait_ge(su2.h, su2_b + ci - 2)
                            for k in range(NCD):
                                ins = t.matmul(
                                    psW1[slot][:],
                                    W1_sb[:, k * DFF + c * 128: k * DFF + (c + 1) * 128],
                                    xn_r3[g % 2][:, k * R3:(k + 1) * R3],
                                    start=(k == 0), stop=(k == NCD - 1),
                                    skip_group_check=True)
                            sh.inc(ins)
                        cw = ci - 2
                        if cw >= 0 and cw % NCF == 0:
                            gg = cw // NCF
                            for tt in range(2):
                                c0 = (gg % NR3) * R3 + tt * 128
                                for h in range(2):
                                    if gg >= 1:
                                        t.wait_ge(soadd.h,
                                                  soadd_b + 2 * (2 * (gg - 1) + tt) + h + 1)
                                    t.matmul(psOut[tt][:, h * 512:(h + 1) * 512],
                                             o_fm2[gg // NR3][:, c0:c0 + 128],
                                             Wo_sb[:, h * 512:(h + 1) * 512],
                                             start=True, stop=False,
                                             skip_group_check=True)
                        if cw >= 0:
                            cc = cw % NCF
                            t.wait_ge(srb.h, srb_b + cw + 1)
                            for tt in range(2):
                                for h in range(2):
                                    ins = t.matmul(
                                        psOut[tt][:, h * 512:(h + 1) * 512],
                                        rb_sb[cw % 3][:, tt * 128:(tt + 1) * 128],
                                        W2_sb[:, cc * D + h * 512: cc * D + (h + 1) * 512],
                                        start=False, stop=(cc == NCF - 1),
                                        skip_group_check=True)
                            sw2.inc(ins)
                    t.drain()
                    t.sem_inc(sbar.h, 1)

          BAR[0] += 4
        except _Done:
            pass

        with nc.Block() as block:
            @block.sync
            def _(sync):
                sync.wait_ge(sc0.h, 128)

            @block.gpsimd
            def _(g):
                g.wait_ge(swq.h, 48)
                g.wait_ge(swr.h, 144)

    return nc


def kernel(L0_attn_x, L0_attn_Wq, L0_attn_Wk, L0_attn_Wv, L0_attn_Wo,
           L0_ffn_W1, L0_ffn_b1, L0_ffn_quantum_phases,
           L0_ffn_quantum_weights_real, L0_ffn_quantum_weights_imag,
           L0_ffn_W2, L0_ffn_b2):
    x = np.ascontiguousarray(np.asarray(L0_attn_x, dtype=np.float32))
    wq = np.ascontiguousarray(np.asarray(L0_attn_Wq, dtype=np.float32))
    wk = np.ascontiguousarray(np.asarray(L0_attn_Wk, dtype=np.float32))
    wv = np.ascontiguousarray(np.asarray(L0_attn_Wv, dtype=np.float32))
    wo = np.ascontiguousarray(np.asarray(L0_attn_Wo, dtype=np.float32))
    w1 = np.ascontiguousarray(np.asarray(L0_ffn_W1, dtype=np.float32))
    w2 = np.ascontiguousarray(np.asarray(L0_ffn_W2, dtype=np.float32))
    b1 = np.asarray(L0_ffn_b1, dtype=np.float32)
    th = np.asarray(L0_ffn_quantum_phases, dtype=np.float32)
    wr = np.asarray(L0_ffn_quantum_weights_real, dtype=np.float32)
    wi = np.asarray(L0_ffn_quantum_weights_imag, dtype=np.float32)
    b2 = np.asarray(L0_ffn_b2, dtype=np.float32)

    per_feat = lambda a: np.ascontiguousarray(a.reshape(NCF, 128).T)
    b1t = per_feat(b1)
    tht = per_feat(th)
    wrt = per_feat(wr)
    wit = per_feat(wi)
    b2rep = np.ascontiguousarray(np.broadcast_to(b2.reshape(1, D),
                                                 (128, D))).astype(np.float32)
    maskd = np.where(np.triu(np.ones((128, 128), dtype=bool), k=1),
                     np.float32(-1e9), np.float32(0.0)).astype(ml_dtypes.bfloat16)
    idn = np.eye(128, dtype=ml_dtypes.bfloat16)
    ones1 = np.ones((1, 128), dtype=ml_dtypes.bfloat16)
    fcst = np.zeros((128, 8), dtype=np.float32)
    fcst[:, 0] = LNEPS
    fcst[:, 1] = -BS

    nc = build_nc()
    in_maps = []
    for i in range(NCORE):
        xc = np.ascontiguousarray(x[NSEQ * i:NSEQ * (i + 1)].reshape(NTOK, D))
        in_maps.append({
            "x": xc, "w1": w1, "w2": w2, "wq": wq, "wk": wk, "wv": wv, "wo": wo,
            "b1t": b1t, "tht": tht, "wrt": wrt, "wit": wit, "b2rep": b2rep,
            "maskd": maskd, "idn": idn, "ones1": ones1, "fcst": fcst,
        })
    res = run_bass_kernel_spmd(nc, in_maps, core_ids=list(range(NCORE)))
    outs = [r["out"].reshape(NSEQ, T, D) for r in res.results]
    return np.ascontiguousarray(np.concatenate(outs, axis=0))


# version 39
# speedup vs baseline: 1.0213x; 1.0051x over previous
import os
import numpy as np
import ml_dtypes
from contextlib import ExitStack

import concourse.bass as bass
import concourse.mybir as mybir
from concourse.bass_utils import run_bass_kernel_spmd

F32 = mybir.dt.float32
BF16 = mybir.dt.bfloat16
AF = mybir.ActivationFunctionType
ALU = mybir.AluOpType
AX = mybir.AxisListType

NCORE = 8
T = 2048
D = 1024
DK = 128
DFF = 4096
NSEQ = 2
NTOK = NSEQ * T
NCD = D // 128
NCF = DFF // 128
NT1 = T // 128
R1 = 512
NR1 = T // R1
R3 = 256
NR3 = T // R3
QSCALE = float(DK) ** -0.5
BS = 12.0
LNEPS = 1e-5


class Ctr:

    def __init__(self, h):
        self.h = h
        self.n = 0

    def inc(self, ins, k=1):
        ins.then_inc(self.h, k)
        self.n += k
        return self.n


def build_nc():
    nc = bass.Bass("TRN2", target_bir_lowering=False)

    x_in = nc.dram_tensor("x", [NTOK, D], F32, kind="ExternalInput").ap()
    w1 = nc.dram_tensor("w1", [D, DFF], F32, kind="ExternalInput").ap()
    w2 = nc.dram_tensor("w2", [DFF, D], F32, kind="ExternalInput").ap()
    wq = nc.dram_tensor("wq", [D, DK], F32, kind="ExternalInput").ap()
    wk = nc.dram_tensor("wk", [D, DK], F32, kind="ExternalInput").ap()
    wv = nc.dram_tensor("wv", [D, DK], F32, kind="ExternalInput").ap()
    wo = nc.dram_tensor("wo", [DK, D], F32, kind="ExternalInput").ap()
    b1t = nc.dram_tensor("b1t", [128, NCF], F32, kind="ExternalInput").ap()
    tht = nc.dram_tensor("tht", [128, NCF], F32, kind="ExternalInput").ap()
    wrt = nc.dram_tensor("wrt", [128, NCF], F32, kind="ExternalInput").ap()
    wit = nc.dram_tensor("wit", [128, NCF], F32, kind="ExternalInput").ap()
    b2rep = nc.dram_tensor("b2rep", [128, D], F32, kind="ExternalInput").ap()
    maskd = nc.dram_tensor("maskd", [128, 128], BF16, kind="ExternalInput").ap()
    idn = nc.dram_tensor("idn", [128, 128], BF16, kind="ExternalInput").ap()
    ones1 = nc.dram_tensor("ones1", [1, 128], BF16, kind="ExternalInput").ap()
    fcst = nc.dram_tensor("fcst", [128, 8], F32, kind="ExternalInput").ap()
    out = nc.dram_tensor("out", [NTOK, D], F32, kind="ExternalOutput").ap()
    _xnkind = "ExternalOutput" if os.environ.get("KDBG_XN") else "Internal"
    xn_scr = [nc.dram_tensor(f"xn_scr{i}", [128, NCD, T], BF16, kind=_xnkind).ap()
              for i in range(NSEQ)]

    with ExitStack() as ctx:
        sbuf = lambda name, shape, dt: ctx.enter_context(nc.sbuf_tensor(name, shape, dt))
        sem = lambda name: Ctr(ctx.enter_context(nc.semaphore(name)))

        W1_sb = sbuf("W1_sb", [128, NCD * DFF], BF16)
        W2_sb = sbuf("W2_sb", [128, NCF * D], BF16)
        Wq_sb = sbuf("Wq_sb", [128, NCD * DK], BF16)
        Wk_sb = sbuf("Wk_sb", [128, NCD * DK], BF16)
        Wv_sb = sbuf("Wv_sb", [128, NCD * DK], BF16)
        Wo_sb = sbuf("Wo_sb", [128, D], BF16)
        b1_sb = sbuf("b1_sb", [128, NCF], F32)
        th_sb = sbuf("th_sb", [128, NCF], F32)
        wr_sb = sbuf("wr_sb", [128, NCF], F32)
        wi_sb = sbuf("wi_sb", [128, NCF], F32)
        Rp_sb = sbuf("Rp_sb", [128, NCF], F32)
        phb_sb = sbuf("phb_sb", [128, NCF], F32)
        prep = sbuf("prep", [128, 8 * NCF], F32)
        mask_sb = sbuf("mask_sb", [128, 128], BF16)
        idn_sb = sbuf("idn_sb", [128, 128], BF16)
        ones_sb = sbuf("ones_sb", [1, 128], BF16)
        diag_li = [sbuf(f"diag_li{i}", [128, 128], BF16) for i in range(2)]
        fc_sb = sbuf("fc_sb", [128, 8], F32)
        stats = sbuf("stats", [128, 64], F32)

        x_sb = [sbuf(f"x_sb{i}", [128, D], F32) for i in range(3)]

        xn_rg = sbuf("xn_rg", [128, NCD * R1], BF16)
        Q_fm = sbuf("Q_fm", [128, T], BF16)
        b2q = Q_fm[:].bitcast(F32)
        K_fm = sbuf("K_fm", [128, T], BF16)
        V_tm = sbuf("V_tm", [128, NT1 * DK], BF16)
        o_fm2 = [sbuf(f"o_fm{i}", [128, T], BF16) for i in range(2)]
        probs = sbuf("probs", [128, T], BF16)
        xn_tm = [probs[:, 0:D], probs[:, D:2 * D], None]
        pT_big = [sbuf(f"pT_big{i}", [128, 1024], BF16) for i in range(2)]
        xn_tm[2] = pT_big[1][:]
        xn_r3 = [sbuf(f"xn_r3{i}", [128, NCD * R3], BF16) for i in range(2)]
        x_sb.append(xn_r3[0][:].bitcast(F32))
        x_sb.append(xn_r3[1][:].bitcast(F32))
        rb_sb = [sbuf(f"rb_sb{i}", [128, R3], BF16) for i in range(3)]
        ss_sb = [sbuf(f"ss_sb{i}", [128, R3], BF16) for i in range(2)]
        u2_sb = [sbuf(f"u2_sb{i}", [128, R3], F32) for i in range(2)]
        out_sb = sbuf("out_sb", [128, D], F32)
        out_sb2 = [out_sb, probs[:].bitcast(F32)]

        sc0 = sem("sc0")
        swq = sem("swq")
        swr = sem("swr")
        sbar = sem("sbar")
        sxp = [sem(f"sxp{i}") for i in range(5)]
        ssq = sem("ssq")
        svar = sem("svar")
        srst = sem("srst")
        sln = sem("sln")
        sdg3 = sem("sdg3")
        stx = sem("stx")
        stxe = sem("stxe")
        sxnst = [sem(f"sxnst{i}") for i in range(4)]
        sqk = sem("sqk")
        sqke = sem("sqke")
        slg = sem("slg")
        smd = sem("smd")
        se1 = sem("se1")
        se1B = sem("se1B")
        se2 = sem("se2")
        stx2 = sem("stx2")
        slgA = sem("slgA")
        st2e = sem("st2e")
        st2eV = sem("st2eV")
        spv = sem("spv")
        soe = sem("soe")
        sxn3p = [sem("sxn3p0"), sem("sxn3p1")]
        sh = sem("sh")
        ssc3 = sem("ssc3")
        su2 = sem("su2")
        srb = sem("srb")
        sw2 = sem("sw2")
        soadd = sem("soadd")
        sost = [[sem(f"sost{h}{p}") for p in range(2)] for h in range(2)]
        sp1 = sem("sp1")
        sp2 = sem("sp2")
        sp3 = sem("sp3")
        sb2 = sem("sb2")
        sprep = sem("sprep")

        BAR = [0]
        _plim = int(os.environ.get("KDBG_PHASES", "99"))

        class _Done(Exception):
            pass

        with nc.Block(no_gpsimd_drain=True) as block:
            @block.sync
            def _(sync):
                for dst, src in [(fc_sb, fcst), (idn_sb, idn), (mask_sb, maskd),
                                 (ones_sb, ones1), (b1_sb, b1t), (th_sb, tht),
                                 (wr_sb, wrt), (wi_sb, wit)]:
                    sc0.inc(sync.dma_start(dst[:], src), 16)

            @block.gpsimd
            def _(g):
                for w_hbm, w_sb in [(wq, Wq_sb), (wk, Wk_sb), (wv, Wv_sb)]:
                    src = w_hbm.rearrange("(c p) k -> p c k", c=NCD, p=128)
                    dst = w_sb[:].rearrange("p (c k) -> p c k", c=NCD)
                    swq.inc(g.dma_start(dst, src), 16)
                swr.inc(g.dma_start(Wo_sb[:], wo), 16)


        try:
          for s in range(NSEQ):
            xb = s * T
            if 3 * s + 1 > _plim:
                raise _Done()
            with ExitStack() as pctx:
                psa = lambda name, shape, dt: pctx.enter_context(
                    nc.psum_tensor(name, shape, dt))
                psTxB = [psa(f"psTx{s}_{i}", [128, 1024], F32) for i in range(2)]
                psQ = psa(f"psQ{s}", [128, R1], F32)
                psK = psa(f"psK{s}", [128, R1], F32)
                psVB = psa(f"psVB{s}", [128, R1], F32)
                blk = pctx.enter_context(nc.Block(no_gpsimd_drain=True))

                sxp_b = [sxp[i].n for i in range(5)]
                ssq_b = ssq.n
                svar_b = svar.n
                srst_b = srst.n
                sln_b = sln.n
                sdg3_b = sdg3.n
                stx_b = stx.n
                stxe_b = stxe.n
                sxnst_b = [sxnst[i].n for i in range(4)]
                sqk_b = sqk.n
                sqke_b = sqke.n
                if s == 0:
                    sxn3p_pre_seq = {}
                bar_in = BAR[0]

                @blk.sync
                def _(sync):
                    if bar_in:
                        sync.wait_ge(sbar.h, bar_in)
                    for j in range(NT1):
                        if j >= 5:
                            sync.wait_ge(sln.h, sln_b + j - 4)
                            sync.wait_ge(svar.h, svar_b + j - 4)
                        sxp[j % 5].inc(sync.dma_start(
                            x_sb[j % 5][:], x_in[xb + j * 128: xb + (j + 1) * 128, :]), 16)
                    for pp in range(5):
                        sync.wait_ge(sxp[pp].h, sxp_b[pp] + 16 * len(
                            [j for j in range(NT1) if j % 5 == pp]))
                    sync.sem_inc(sbar.h, 1)

                @blk.gpsimd
                def _(g):
                    if bar_in:
                        g.wait_ge(sbar.h, bar_in)
                    for j in range(NT1):
                        g.wait_ge(stxe.h, stxe_b + j + 1)
                        src = xn_rg[:, (j % 4) * 1024:(j % 4 + 1) * 1024].rearrange(
                            "p (c n) -> p c n", c=NCD)
                        dst = xn_scr[s][:, :, j * 128:(j + 1) * 128]
                        sxnst[j % 4].inc(g.dma_start(dst, src), 16)
                    if s == 0:
                        for jw in range(4):
                            srcw = w1.rearrange("(a c p) f -> a p c f",
                                                a=4, c=2, p=128)[jw]
                            dstw = W1_sb[:, jw * 2 * DFF:(jw + 1) * 2 * DFF].rearrange(
                                "p (c f) -> p c f", c=2)
                            swr.inc(g.dma_start(dstw, srcw), 16)
                    else:
                        for jw in range(4):
                            srcw = w2.rearrange("(a c p) d -> a p c d",
                                                a=4, c=8, p=128)[jw]
                            dstw = W2_sb[:, jw * 8 * D:(jw + 1) * 8 * D].rearrange(
                                "p (c d) -> p c d", c=8)
                            swr.inc(g.dma_start(dstw, srcw), 16)
                    for i in range(4):
                        g.wait_ge(sxnst[i].h, sxnst_b[i] + 16 * NR1)
                    g.sem_inc(sbar.h, 1)

                NF = NCF
                col = lambda i: prep[:, i * NF:(i + 1) * NF]
                pv = {}
                if s == 0:
                    vops = [lambda v: v.wait_ge(sp1.h, 2)]

                    def _o(fn):
                        vops.append(lambda v, fn=fn: (v.drain(), fn(v)))
                    _o(lambda v: v.tensor_scalar(col(1), col(6), -2.0, 1.0,
                                                 ALU.mult, ALU.add))
                    _o(lambda v: v.tensor_copy(col(0), col(5)))
                    for _ in range(4):
                        _o(lambda v: v.scalar_tensor_tensor(col(2), col(0), 2.0,
                                                            col(1), ALU.mult, ALU.mult))
                        _o(lambda v: v.tensor_tensor(col(3), col(0), col(0), ALU.mult))
                        _o(lambda v: v.tensor_scalar(col(1), col(3), -2.0, 1.0,
                                                     ALU.mult, ALU.add))
                        _o(lambda v: v.tensor_copy(col(0), col(2)))
                    _o(lambda v: v.tensor_tensor(col(2), col(1), wr_sb[:], ALU.mult))
                    _o(lambda v: v.tensor_tensor(col(3), col(0), wi_sb[:], ALU.mult))
                    _o(lambda v: v.tensor_tensor(col(2), col(2), col(3), ALU.add))
                    _o(lambda v: v.tensor_tensor(col(3), col(2), wi_sb[:], ALU.mult))
                    _o(lambda v: v.tensor_scalar(col(4), col(1), -1.0, None, ALU.mult))
                    _o(lambda v: v.tensor_tensor(col(4), col(1), col(4), ALU.max))
                    _o(lambda v: v.tensor_scalar(col(4), col(4), 1.0, None, ALU.add))
                    _o(lambda v: v.reciprocal(col(5), col(4)))
                    _o(lambda v: sp2.inc(v.tensor_tensor(col(4), col(0), col(5),
                                                         ALU.mult)))
                    for i, fn in enumerate(vops):
                        pv.setdefault(min(i // 3, 9), []).append(fn)

                    def _mid(v):
                        v.wait_ge(sp3.h, 1)
                        v.drain()
                        sp2.inc(v.tensor_tensor(col(5), col(4), col(7), ALU.mult))
                    pv.setdefault(13, []).append(_mid)

                    def _tail(v):
                        v.wait_ge(sp3.h, 2)
                        v.drain()
                        v.scalar_tensor_tensor(Rp_sb[:], col(3), 0.1, col(7),
                                               ALU.mult, ALU.mult)
                        v.drain()
                        v.tensor_scalar(col(5), col(6), 2.0, None, ALU.mult)
                        v.drain()
                        ins = v.scalar_tensor_tensor(phb_sb[:], b1_sb[:], 0.1,
                                                     col(5), ALU.mult, ALU.add)
                        sprep.inc(ins)
                    pv.setdefault(15, []).append(_tail)

                @blk.vector
                def _(v):
                    if bar_in:
                        v.wait_ge(sbar.h, bar_in)
                    for j in range(NT1 + 2):
                        if j < NT1:
                            p2 = 16 * (j % 4)
                            v.wait_ge(sxp[j % 5].h, sxp_b[j % 5] + 16 * (j // 5 + 1))
                            v.bn_stats(stats[:, p2 + 0:p2 + 6], x_sb[j % 5][:, 0:512])
                            v.bn_stats(stats[:, p2 + 6:p2 + 12], x_sb[j % 5][:, 512:1024])
                            v.drain()
                            ins = v.bn_aggr(stats[:, p2 + 12:p2 + 14],
                                            stats[:, p2 + 0:p2 + 12])
                            svar.inc(ins)
                        jj = j - 2
                        if jj >= 0:
                            p2 = 16 * (jj % 4)
                            v.wait_ge(srst.h, srst_b + jj + 1)
                            v.reciprocal(stats[:, p2 + 14:p2 + 15],
                                         stats[:, p2 + 15:p2 + 16])
                            if jj >= 2:
                                v.wait_ge(stx.h, stx_b + jj - 1)
                            v.drain()
                            ins = v.tensor_scalar(
                                diag_li[jj % 2][:], idn_sb[:],
                                stats[:, p2 + 14:p2 + 15], -1.0,
                                ALU.mult, ALU.mult)
                            sdg3.inc(ins)
                        if j < NT1:
                            for fn in pv.get(j, []):
                                fn(v)
                    v.drain()
                    v.sem_inc(sbar.h, 1)

                @blk.scalar
                def _(sc):
                    if bar_in:
                        sc.wait_ge(sbar.h, bar_in)
                    sc.wait_ge(sc0.h, 128)
                    if s == 0:
                        sc.activation(col(5), th_sb[:], AF.Sin, scale=1.0 / 16)
                        sc.activation(col(4), th_sb[:], AF.Sin, scale=1.0 / 32)
                        sc.drain()
                        sp1.inc(sc.activation(col(6), col(4), AF.Square), 2)
                    for j in range(NT1 + 2):
                        jj = j - 1
                        if s == 0 and j == 12:
                            sc.wait_ge(sp2.h, 1)
                            sc.drain()
                            sp3.inc(sc.sign(col(7), col(1)))
                        if s == 0 and j == 14:
                            sc.wait_ge(sp2.h, 2)
                            sc.drain()
                            sp3.inc(sc.activation(col(6), col(5), AF.Arctan))
                        if 0 <= jj < NT1:
                            p2 = 16 * (jj % 4)
                            sc.wait_ge(svar.h, svar_b + jj + 1)
                            srst.inc(sc.activation(stats[:, p2 + 15:p2 + 16],
                                                   stats[:, p2 + 13:p2 + 14],
                                                   AF.Sqrt, bias=fc_sb[:, 0:1]))
                            if jj >= 3:
                                sc.wait_ge(stx.h, stx_b + jj - 2)
                            ins = sc.activation(xn_tm[jj % 3], x_sb[jj % 5][:],
                                                AF.Identity, scale=-1.0,
                                                bias=stats[:, p2 + 12:p2 + 13])
                            sln.inc(ins)
                        je = j - 2
                        if 0 <= je < NT1:
                            r, tt = je // 4, je % 4
                            if tt == 0 and r > 0:
                                sc.wait_ge(sqk.h, sqk_b + 4 * r)
                            if je >= 4:
                                sc.wait_ge(sxnst[je % 4].h,
                                           sxnst_b[je % 4] + 16 * (je // 4))
                            sc.wait_ge(stx.h, stx_b + je + 1)
                            stxe.inc(sc.copy(xn_rg[:, tt * 1024:(tt + 1) * 1024],
                                             psTxB[je % 2][:]))
                            if tt == 0 and r >= 1:
                                rp = r - 1
                                sc.wait_ge(sqk.h, sqk_b + 4 * (rp + 1))
                                sqke.inc(sc.activation(Q_fm[:, rp * R1:(rp + 1) * R1],
                                                       psQ[:], AF.Copy, scale=QSCALE))
                                sqke.inc(sc.copy(K_fm[:, rp * R1:(rp + 1) * R1], psK[:]))
                                sqke.inc(sc.copy(V_tm[:, rp * R1:(rp + 1) * R1], psVB[:]))

                    rp = NR1 - 1
                    sc.wait_ge(sqk.h, sqk_b + 4 * (rp + 1))
                    sqke.inc(sc.activation(Q_fm[:, rp * R1:(rp + 1) * R1],
                                           psQ[:], AF.Copy, scale=QSCALE))
                    sqke.inc(sc.copy(K_fm[:, rp * R1:(rp + 1) * R1], psK[:]))
                    sqke.inc(sc.copy(V_tm[:, rp * R1:(rp + 1) * R1], psVB[:]))
                    sc.drain()
                    sc.sem_inc(sbar.h, 1)

                @blk.tensor
                def _(t):
                    if bar_in:
                        t.wait_ge(sbar.h, bar_in)
                    t.wait_ge(sc0.h, 128)
                    if s == 0:
                        t.wait_ge(swq.h, 48)
                    def qkv_slice(jq):
                        tt, r = jq % 4, jq // 4
                        if tt == 0 and r > 0:
                            t.wait_ge(sqke.h, sqke_b + 3 * r)
                        t.wait_ge(stxe.h, stxe_b + jq + 1)
                        for ci in range(NCD):
                            t.matmul(psQ[:, tt * 128:(tt + 1) * 128],
                                     Wq_sb[:, ci * DK:(ci + 1) * DK],
                                     xn_rg[:, tt * 1024 + ci * 128:
                                            tt * 1024 + (ci + 1) * 128],
                                     start=(ci == 0), stop=(ci == NCD - 1),
                                     skip_group_check=True)
                        for ci in range(NCD):
                            t.matmul(psK[:, tt * 128:(tt + 1) * 128],
                                     Wk_sb[:, ci * DK:(ci + 1) * DK],
                                     xn_rg[:, tt * 1024 + ci * 128:
                                            tt * 1024 + (ci + 1) * 128],
                                     start=(ci == 0), stop=(ci == NCD - 1),
                                     skip_group_check=True)
                        for ci in range(NCD):
                            ins = t.matmul(psVB[:, tt * 128:(tt + 1) * 128],
                                           xn_rg[:, tt * 1024 + ci * 128:
                                                  tt * 1024 + (ci + 1) * 128],
                                           Wv_sb[:, ci * DK:(ci + 1) * DK],
                                           start=(ci == 0), stop=(ci == NCD - 1),
                                           skip_group_check=True)
                        sqk.inc(ins)

                    for j in range(NT1):
                        t.wait_ge(sln.h, sln_b + j + 1)
                        t.wait_ge(sdg3.h, sdg3_b + j + 1)
                        if j >= 2:
                            t.wait_ge(stxe.h, stxe_b + j - 1)
                        for c in range(NCD):
                            ins = t.matmul(psTxB[j % 2][:, c * 128:(c + 1) * 128],
                                           xn_tm[j % 3][:, c * 128:(c + 1) * 128],
                                           diag_li[j % 2][:],
                                           start=True, stop=True,
                                           skip_group_check=True)
                        stx.inc(ins)
                        if j >= 1:
                            qkv_slice(j - 1)
                    qkv_slice(NT1 - 1)
                    t.drain()
                    t.sem_inc(sbar.h, 1)

            BAR[0] += 5
            if 3 * s + 2 > _plim:
                raise _Done()

            with ExitStack() as pctx:
                psa = lambda name, shape, dt: pctx.enter_context(
                    nc.psum_tensor(name, shape, dt))
                psLogA = psa(f"psLogA{s}", [128, 1024], F32)
                psLogB = psa(f"psLogB{s}", [128, 1024], F32)
                psT2 = [psa(f"psT2{s}_{i}", [128, 512], F32) for i in range(2)]
                psO = [psa(f"psO{s}_{i}", [128, 128], F32) for i in range(2)]
                blk = pctx.enter_context(nc.Block())

                slg_b = slg.n
                se1_b = se1.n
                se1B_b = se1B.n
                se2_b = se2.n
                stx2_b = stx2.n
                st2e_b = st2e.n
                st2eV_b = st2eV.n
                spv_b = spv.n
                soe_b = soe.n
                bar_in = BAR[0]
                cum = [0]
                for qi in range(NT1):
                    cum.append(cum[-1] + qi + 1)

                slgA_b = slgA.n
                p2buf = [xn_rg[:, 0:T], xn_rg[:, T:2 * T]]
                nsg = [j // 4 + 1 for j in range(NT1)]
                cum_sg = [0]
                for j in range(NT1):
                    cum_sg.append(cum_sg[-1] + nsg[j])
                segs = [(j, si) for j in range(NT1) for si in range(nsg[j])]
                bank_of, ord_in_bank = {}, {}
                _cnt = [0, 0]
                for g, key in enumerate(segs):
                    b = g % 2
                    bank_of[key] = b
                    _cnt[b] += 1
                    ord_in_bank[key] = _cnt[b]
                segw2 = lambda j, si: (min(j + 1 - si * 4, 4)) * 128
                ev_sem = lambda b: (st2e, st2e_b) if b == 0 else (st2eV, st2eV_b)

                @blk.tensor
                def _(t):
                    t.wait_ge(sbar.h, bar_in)
                    for qq in range(NT1 + 2):
                        j = qq - 1
                        if qq < NT1:
                            qi = qq
                            if qi >= 1:
                                t.wait_ge(se1.h, se1_b + qi)
                            qsl = Q_fm[:, qi * 128:(qi + 1) * 128]
                            if qi >= 8:
                                for kb in range(2):
                                    ins = t.matmul(psLogA[:, kb * R1:(kb + 1) * R1],
                                                   qsl, K_fm[:, kb * R1:(kb + 1) * R1],
                                                   start=True, stop=True)
                                slgA.inc(ins)
                                if qi >= 9:
                                    t.wait_ge(se1B.h, se1B_b + qi - 8)
                            qb = qi - 8 if qi >= 8 else qi
                            ps_h = psLogB if qi >= 8 else psLogA
                            cb = 1024 if qi >= 8 else 0
                            nf = qb // 4
                            for kb in range(nf):
                                t.matmul(ps_h[:, kb * R1:(kb + 1) * R1], qsl,
                                         K_fm[:, cb + kb * R1: cb + (kb + 1) * R1],
                                         start=True, stop=True)
                            for m in range(qb % 4 + 1):
                                c0 = nf * R1 + m * 128
                                t.matmul(ps_h[:, c0:c0 + 128], qsl,
                                         K_fm[:, cb + c0: cb + c0 + 128],
                                         start=(m == 0), stop=False,
                                         skip_group_check=True)
                            ins = t.matmul(ps_h[:, qb * 128:(qb + 1) * 128],
                                           idn_sb[:], mask_sb[:],
                                           start=False, stop=True,
                                           skip_group_check=True)
                            slg.inc(ins)
                        jv = qq - 2
                        if jv >= 0:
                            if jv >= 2:
                                t.wait_ge(soe.h, soe_b + jv - 1)
                            for si in range(nsg[jv]):
                                sem, base = ev_sem(bank_of[(jv, si)])
                                t.wait_ge(sem.h, base + ord_in_bank[(jv, si)])
                                for c in range(si * 4, min(jv + 1, si * 4 + 4)):
                                    ins = t.matmul(psO[jv % 2][:],
                                                   V_tm[:, c * DK:(c + 1) * DK],
                                                   pT_big[c // 8][:, (c % 8) * 128:(c % 8 + 1) * 128],
                                                   start=(c == 0), stop=(c == jv),
                                                   skip_group_check=True)
                            spv.inc(ins)
                        if 0 <= j < NT1:
                            t.wait_ge(se1.h, se1_b + j + 1)
                            if j >= 8:
                                t.wait_ge(se1B.h, se1B_b + j - 7)
                            t.wait_ge(se2.h, se2_b + j + 1)
                            for si in range(nsg[j]):
                                g = cum_sg[j] + si
                                b = g % 2
                                if g >= 2:
                                    pk = segs[g - 2]
                                    sem, base = ev_sem(bank_of[pk])
                                    t.wait_ge(sem.h, base + ord_in_bank[pk])
                                for c in range(si * 4, min(j + 1, si * 4 + 4)):
                                    ins = t.matmul(
                                        psT2[b][:, (c % 4) * 128:(c % 4 + 1) * 128],
                                        p2buf[j % 2][:, c * 128:(c + 1) * 128],
                                        diag_li[j % 2][:],
                                        start=True, stop=True,
                                        skip_group_check=True)
                                stx2.inc(ins)
                    t.drain()
                    t.sem_inc(sbar.h, 1)

                if s == 1:
                    sxn3p_pre_seq[0] = [sxn3p[0].n, sxn3p[1].n]

                    @blk.sync
                    def _(sync):
                        sync.wait_ge(sbar.h, bar_in)
                        for gg in range(2):
                            srcp = xn_scr[0][:, :, gg * R3:(gg + 1) * R3]
                            dstp = xn_r3[gg][:].rearrange("p (c n) -> p c n", c=NCD)
                            sxn3p[gg].inc(sync.dma_start(dstp, srcp), 16)
                        sync.sem_inc(sbar.h, 1)

                @blk.vector
                def _(v):
                    v.wait_ge(sbar.h, bar_in)
                    v.wait_ge(sc0.h, 128)
                    for j in range(NT1):
                        v.wait_ge(se1.h, se1_b + j + 1)
                        if j >= 8:
                            v.wait_ge(se1B.h, se1B_b + j - 7)
                        v.drain()
                        a0 = 16 + 4 * (j % 2)
                        if j >= 8:
                            v.tensor_tensor(stats[:, 22:23], stats[:, a0:a0 + 1],
                                            stats[:, a0 + 1:a0 + 2], ALU.add)
                        else:
                            v.tensor_copy(stats[:, 22:23], stats[:, a0:a0 + 1])
                        v.drain()
                        v.reciprocal(stats[:, 23:24], stats[:, 22:23])
                        if j >= 2:
                            v.wait_ge(stx2.h, stx2_b + cum_sg[j - 1])
                        v.drain()
                        ins = v.tensor_scalar(diag_li[j % 2][:], idn_sb[:],
                                              stats[:, 23:24], None, ALU.mult)
                        se2.inc(ins)
                        for si in range(nsg[j]):
                            b = bank_of[(j, si)]
                            v.wait_ge(stx2.h, stx2_b + cum_sg[j] + si + 1)
                            if j >= 1:
                                v.wait_ge(spv.h, spv_b + j)
                            v.drain()
                            w = segw2(j, si)
                            (st2e if b == 0 else st2eV).inc(v.tensor_copy(
                                pT_big[si // 2][:, (si % 2) * 512:(si % 2) * 512 + w],
                                psT2[b][:, 0:w]))
                    v.drain()
                    v.sem_inc(sbar.h, 1)

                @blk.scalar
                def _(sc):
                    sc.wait_ge(sbar.h, bar_in)
                    for qq in range(NT1 + 2):
                        j = qq - 1
                        if qq < NT1:
                            qi = qq
                            if qi >= 2:
                                sc.wait_ge(stx2.h, stx2_b + cum_sg[qi - 1])
                            sc.drain()
                            if qi >= 8:
                                sc.wait_ge(slgA.h, slgA_b + qi - 7)
                                a0 = 16 + 4 * (qi % 2)
                                ins = sc.activation(p2buf[qi % 2][:, 0:1024],
                                                    psLogA[:], AF.Exp,
                                                    bias=fc_sb[:, 1:2],
                                                    accum_out=stats[:, a0:a0 + 1])
                                se1.inc(ins)
                                sc.wait_ge(slg.h, slg_b + qi + 1)
                                sc.drain()
                                ins = sc.activation(
                                    p2buf[qi % 2][:, 1024:(qi + 1) * 128],
                                    psLogB[:, 0:(qi - 7) * 128], AF.Exp,
                                    bias=fc_sb[:, 1:2],
                                    accum_out=stats[:, a0 + 1:a0 + 2])
                                se1B.inc(ins)
                            else:
                                sc.wait_ge(slg.h, slg_b + qi + 1)
                                a0 = 16 + 4 * (qi % 2)
                                ins = sc.activation(p2buf[qi % 2][:, 0:(qi + 1) * 128],
                                                    psLogA[:, 0:(qi + 1) * 128],
                                                    AF.Exp, bias=fc_sb[:, 1:2],
                                                    accum_out=stats[:, a0:a0 + 1])
                                se1.inc(ins)
                        j2 = qq - 2
                        if j2 >= 0:
                            sc.wait_ge(spv.h, spv_b + j2 + 1)
                            soe.inc(sc.copy(o_fm2[s][:, j2 * 128:(j2 + 1) * 128],
                                            psO[j2 % 2][:]))
                    sc.drain()
                    sc.sem_inc(sbar.h, 1)

            BAR[0] += 3 + (1 if s == 1 else 0)

          if 3 > _plim:
            raise _Done()

          with ExitStack() as pctx:
                psa = lambda name, shape, dt: pctx.enter_context(
                    nc.psum_tensor(name, shape, dt))
                psW1 = [psa(f"psW1_{i}", [128, R3], F32) for i in range(3)]
                psOut = [psa(f"psOut_{i}", [128, D], F32) for i in range(2)]
                blk = pctx.enter_context(nc.Block())

                NRG = 2 * NR3
                NTJ = 2 * 2 * NR3
                NCHG = NRG * NCF
                sxn3p_b = sxn3p_pre_seq[0]
                sxp_b = [sxp[i].n for i in range(3)]
                sh_b = sh.n
                ssc3_b = ssc3.n
                su2_b = su2.n
                srb_b = srb.n
                sw2_b = sw2.n
                soadd_b = soadd.n
                sost_b = [[sost[h][p].n for p in range(2)] for h in range(2)]
                sxnst_b3 = [sxnst[i].n for i in range(4)]
                bar_in = BAR[0]

                @blk.sync
                def _(sync):
                    sync.wait_ge(sbar.h, bar_in)
                    for i in range(4):
                        sync.wait_ge(sxnst[i].h, sxnst_b3[i])
                    sb2.inc(sync.dma_start(b2q, b2rep), 16)

                    def xn_load(g):
                        srcp = xn_scr[g // NR3][:, :, (g % NR3) * R3:
                                               (g % NR3 + 1) * R3]
                        dstp = xn_r3[g % 2][:].rearrange("p (c n) -> p c n", c=NCD)
                        sxn3p[g % 2].inc(sync.dma_start(dstp, srcp), 16)

                    def x_load(j):
                        sxp[j % 3].inc(sync.dma_start(
                            x_sb[j % 3][:], x_in[j * 128:(j + 1) * 128, :]), 16)

                    x_load(0)
                    x_load(1)
                    x_load(2)
                    for j in range(NTJ):
                        r0 = j * 128
                        for hh in range(2):
                            sync.wait_ge(soadd.h, soadd_b + 2 * j + hh + 1)
                            sost[hh][j % 2].inc(sync.dma_start(
                                out[r0:r0 + 128, hh * 512:(hh + 1) * 512],
                                out_sb2[j % 2][:, hh * 512:(hh + 1) * 512]), 16)
                        if j + 3 < NTJ:
                            x_load(j + 3)
                        if j % 2 == 1:
                            g = (j - 1) // 2
                            if g + 2 < NRG:
                                sync.wait_ge(sh.h, sh_b + NCF * (g + 1))
                                xn_load(g + 2)
                    for hh in range(2):
                        for p in range(2):
                            sync.wait_ge(sost[hh][p].h, sost_b[hh][p] + 16 * NR3 * 2)
                    sync.sem_inc(sbar.h, 1)

                @blk.vector
                def _(v):
                    v.wait_ge(sbar.h, bar_in)
                    for g in range(NRG):
                        for c in range(NCF):
                            ci = NCF * g + c
                            slot = ci % 3
                            v.wait_ge(ssc3.h, ssc3_b + ci + 1)
                            if ci >= 2:
                                v.wait_ge(srb.h, srb_b + ci - 1)
                            su2.inc(v.scalar_tensor_tensor(
                                u2_sb[ci % 2][:], ss_sb[ci % 2][:], Rp_sb[:, c:c + 1],
                                psW1[slot][:], ALU.mult, ALU.add))
                            if c == 16:
                                if g == 0:
                                    v.wait_ge(sb2.h, 16)
                                for tt2 in range(2):
                                    j = 2 * g + tt2
                                    v.wait_ge(sxp[j % 3].h,
                                              sxp_b[j % 3] + 16 * (j // 3 + 1))
                                    for hh in range(2):
                                        if j >= 2:
                                            v.wait_ge(sost[hh][j % 2].h,
                                                      sost_b[hh][j % 2] + 16 * (j // 2))
                                        v.tensor_tensor(
                                            out_sb2[j % 2][:, hh * 512:(hh + 1) * 512],
                                            x_sb[j % 3][:, hh * 512:(hh + 1) * 512],
                                            b2q[:, hh * 512:(hh + 1) * 512], ALU.add)
                        for tt in range(2):
                            j = 2 * g + tt
                            v.wait_ge(sw2.h, sw2_b + NCF * (g + 1))
                            v.drain()
                            for hh in range(2):
                                soadd.inc(v.tensor_tensor(
                                    out_sb2[j % 2][:, hh * 512:(hh + 1) * 512],
                                    psOut[tt][:, hh * 512:(hh + 1) * 512],
                                    out_sb2[j % 2][:, hh * 512:(hh + 1) * 512],
                                    ALU.add))
                    v.drain()
                    v.sem_inc(sbar.h, 1)

                @blk.scalar
                def _(sc):
                    sc.wait_ge(sbar.h, bar_in)
                    sc.wait_ge(sprep.h, 1)
                    for g in range(NRG):
                        for c in range(NCF):
                            ci = NCF * g + c
                            slot = ci % 3
                            sc.wait_ge(sh.h, sh_b + ci + 1)
                            ssc3.inc(sc.activation(ss_sb[ci % 2][:], psW1[slot][:],
                                                   AF.Sin, scale=0.1,
                                                   bias=phb_sb[:, c:c + 1]))
                            cp = ci - 1
                            if cp >= 0:
                                sc.wait_ge(su2.h, su2_b + cp + 1)
                                if cp >= 3:
                                    sc.wait_ge(sw2.h, sw2_b + cp - 2)
                                srb.inc(sc.activation(rb_sb[cp % 3][:], u2_sb[cp % 2][:],
                                                      AF.Relu, bias=b1_sb[:, (cp % NCF):(cp % NCF) + 1]))
                    cp = NCHG - 1
                    sc.wait_ge(su2.h, su2_b + cp + 1)
                    sc.wait_ge(sw2.h, sw2_b + cp - 2)
                    srb.inc(sc.activation(rb_sb[cp % 3][:], u2_sb[cp % 2][:],
                                          AF.Relu, bias=b1_sb[:, NCF - 1:NCF]))
                    sc.drain()
                    sc.sem_inc(sbar.h, 1)

                @blk.tensor
                def _(t):
                    t.wait_ge(sbar.h, bar_in)
                    t.wait_ge(swq.h, 48)
                    t.wait_ge(swr.h, 144)
                    t.wait_ge(sc0.h, 128)
                    t.wait_ge(sprep.h, 1)
                    for ci in range(NCHG + 2):
                        if ci < NCHG:
                            g, c = divmod(ci, NCF)
                            if c == 0:
                                t.wait_ge(sxn3p[g % 2].h,
                                          sxn3p_b[g % 2] + 16 * (g // 2 + 1))
                            slot = ci % 3
                            if ci >= 3:
                                t.wait_ge(su2.h, su2_b + ci - 2)
                            for k in range(NCD):
                                ins = t.matmul(
                                    psW1[slot][:],
                                    W1_sb[:, k * DFF + c * 128: k * DFF + (c + 1) * 128],
                                    xn_r3[g % 2][:, k * R3:(k + 1) * R3],
                                    start=(k == 0), stop=(k == NCD - 1),
                                    skip_group_check=True)
                            sh.inc(ins)
                        cw = ci - 2
                        if cw >= 0 and cw % NCF == 0:
                            gg = cw // NCF
                            for tt in range(2):
                                c0 = (gg % NR3) * R3 + tt * 128
                                for h in range(2):
                                    if gg >= 1:
                                        t.wait_ge(soadd.h,
                                                  soadd_b + 2 * (2 * (gg - 1) + tt) + h + 1)
                                    t.matmul(psOut[tt][:, h * 512:(h + 1) * 512],
                                             o_fm2[gg // NR3][:, c0:c0 + 128],
                                             Wo_sb[:, h * 512:(h + 1) * 512],
                                             start=True, stop=False,
                                             skip_group_check=True)
                        if cw >= 0:
                            cc = cw % NCF
                            t.wait_ge(srb.h, srb_b + cw + 1)
                            for tt in range(2):
                                for h in range(2):
                                    ins = t.matmul(
                                        psOut[tt][:, h * 512:(h + 1) * 512],
                                        rb_sb[cw % 3][:, tt * 128:(tt + 1) * 128],
                                        W2_sb[:, cc * D + h * 512: cc * D + (h + 1) * 512],
                                        start=False, stop=(cc == NCF - 1),
                                        skip_group_check=True)
                            sw2.inc(ins)
                    t.drain()
                    t.sem_inc(sbar.h, 1)

          BAR[0] += 4
        except _Done:
            pass

        with nc.Block() as block:
            @block.sync
            def _(sync):
                sync.wait_ge(sc0.h, 128)

            @block.gpsimd
            def _(g):
                g.wait_ge(swq.h, 48)
                g.wait_ge(swr.h, 144)

    return nc


def kernel(L0_attn_x, L0_attn_Wq, L0_attn_Wk, L0_attn_Wv, L0_attn_Wo,
           L0_ffn_W1, L0_ffn_b1, L0_ffn_quantum_phases,
           L0_ffn_quantum_weights_real, L0_ffn_quantum_weights_imag,
           L0_ffn_W2, L0_ffn_b2):
    x = np.ascontiguousarray(np.asarray(L0_attn_x, dtype=np.float32))
    wq = np.ascontiguousarray(np.asarray(L0_attn_Wq, dtype=np.float32))
    wk = np.ascontiguousarray(np.asarray(L0_attn_Wk, dtype=np.float32))
    wv = np.ascontiguousarray(np.asarray(L0_attn_Wv, dtype=np.float32))
    wo = np.ascontiguousarray(np.asarray(L0_attn_Wo, dtype=np.float32))
    w1 = np.ascontiguousarray(np.asarray(L0_ffn_W1, dtype=np.float32))
    w2 = np.ascontiguousarray(np.asarray(L0_ffn_W2, dtype=np.float32))
    b1 = np.asarray(L0_ffn_b1, dtype=np.float32)
    th = np.asarray(L0_ffn_quantum_phases, dtype=np.float32)
    wr = np.asarray(L0_ffn_quantum_weights_real, dtype=np.float32)
    wi = np.asarray(L0_ffn_quantum_weights_imag, dtype=np.float32)
    b2 = np.asarray(L0_ffn_b2, dtype=np.float32)

    per_feat = lambda a: np.ascontiguousarray(a.reshape(NCF, 128).T)
    b1t = per_feat(b1)
    tht = per_feat(th)
    wrt = per_feat(wr)
    wit = per_feat(wi)
    b2rep = np.ascontiguousarray(np.broadcast_to(b2.reshape(1, D),
                                                 (128, D))).astype(np.float32)
    maskd = np.where(np.triu(np.ones((128, 128), dtype=bool), k=1),
                     np.float32(-1e9), np.float32(0.0)).astype(ml_dtypes.bfloat16)
    idn = np.eye(128, dtype=ml_dtypes.bfloat16)
    ones1 = np.ones((1, 128), dtype=ml_dtypes.bfloat16)
    fcst = np.zeros((128, 8), dtype=np.float32)
    fcst[:, 0] = LNEPS
    fcst[:, 1] = -BS

    nc = build_nc()
    in_maps = []
    for i in range(NCORE):
        xc = np.ascontiguousarray(x[NSEQ * i:NSEQ * (i + 1)].reshape(NTOK, D))
        in_maps.append({
            "x": xc, "w1": w1, "w2": w2, "wq": wq, "wk": wk, "wv": wv, "wo": wo,
            "b1t": b1t, "tht": tht, "wrt": wrt, "wit": wit, "b2rep": b2rep,
            "maskd": maskd, "idn": idn, "ones1": ones1, "fcst": fcst,
        })
    res = run_bass_kernel_spmd(nc, in_maps, core_ids=list(range(NCORE)))
    outs = [r["out"].reshape(NSEQ, T, D) for r in res.results]
    return np.ascontiguousarray(np.concatenate(outs, axis=0))
